# version 15
# baseline (speedup 1.0000x reference)
import contextlib
import numpy as np
import ml_dtypes
import concourse.bass as bass
import concourse.mybir as mybir
from concourse.bass_utils import run_bass_kernel_spmd

F32 = mybir.dt.float32
BF16 = mybir.dt.bfloat16
AF = mybir.ActivationFunctionType
ALU = mybir.AluOpType
AX = mybir.AxisListType

T = 4096
D = 1024
NCH = 8
TT = 512
NTT = T // TT
NH = 16
DH = 64
CF = 3
NCONV = (T + 2) // CF
JTOT = NCONV + 1
NJT = (JTOT + 127) // 128
RD = 32
LAT = 256
NE = 7
IE = 2048
EPS = 1e-5

DT_SIZE = {F32: 4, BF16: 2}
ALL_PHASES = ("A", "A2", "B", "CD", "E", "F", "G")


def _host_consts():
    c = {}
    c["ident_bf"] = np.eye(128, dtype=ml_dtypes.bfloat16)
    c["ident_f32"] = np.eye(128, dtype=np.float32)
    inv_freq = 1.0 / 10000.0 ** (np.arange(0, RD, 2, dtype=np.float64) / RD)
    fr = np.arange(T, dtype=np.float64)[:, None] * inv_freq[None, :]
    cos16 = np.cos(fr).astype(np.float32).T
    sin16 = np.sin(fr).astype(np.float32).T
    cosT = np.ones((128, T), dtype=np.float32)
    sinT = np.zeros((128, T), dtype=np.float32)
    for half in (0, 64):
        cosT[half:half + 16] = cos16
        cosT[half + 16:half + 32] = cos16
        sinT[half:half + 16] = -sin16
        sinT[half + 16:half + 32] = sin16
    c["cosT"], c["sinT"] = cosT, sinT
    R = np.zeros((128, 128), dtype=np.float32)
    for half in (0, 64):
        for i in range(16):
            R[half + 16 + i, half + i] = 1.0
            R[half + i, half + 16 + i] = 1.0
    c["Rmat_bf"] = R.astype(ml_dtypes.bfloat16)
    jl = np.arange(128)[:, None]
    ql = np.arange(128)[None, :]
    c["mask_diag_bf"] = (ql >= jl).astype(ml_dtypes.bfloat16)
    part = {}
    kind = np.zeros((NJT, NTT), dtype=np.int8)
    for jt in range(NJT):
        jj = np.arange(jt * 128, jt * 128 + 128)
        for qg in range(NTT):
            qq = np.arange(qg * TT, qg * TT + TT)
            allow = (qq[None, :] >= 3 * jj[:, None]) | (jj[:, None] >= NCONV)
            valid = allow[jj < JTOT]
            if valid.all():
                kind[jt, qg] = 1
            elif not valid.any():
                kind[jt, qg] = 0
            else:
                kind[jt, qg] = 2
                part[(jt, qg)] = allow.astype(ml_dtypes.bfloat16)
    c["outer_kind"], c["outer_masks"] = kind, part
    return c


class _Op:
    __slots__ = ("dom", "idx", "fn", "waits", "marked", "cvec", "tag")


class Prog:
    ENGS = ("PE", "DVE", "ACT", "POOL", "SP")
    QUEUES = ("QSP", "QACT", "QPOOL")
    QISSUE = {"QSP": "SP", "QACT": "ACT", "QPOOL": "POOL"}
    NCHAN = {"QSP": 12, "QACT": 8, "QPOOL": 4}

    def __init__(self):
        self.qdoms = tuple(f"{q}#{i}" for q in self.QUEUES
                           for i in range(self.NCHAN[q]))
        self.streams = {e: [] for e in self.ENGS}
        self.counts = {d: 0 for d in self.ENGS + self.qdoms}
        self.eng_clock = {e: {} for e in self.ENGS}
        self.res = {}
        self.ops = {d: [] for d in self.ENGS + self.qdoms}
        self.sems = {}
        self.rr = {q: 0 for q in self.QUEUES}
        self.last_dma = {}

    def op(self, where, fn, reads=(), writes=(), tag=""):
        if where in self.QUEUES:
            ch = self.rr[where]
            self.rr[where] = (ch + 1) % self.NCHAN[where]
            dom, eng = f"{where}#{ch}", self.QISSUE[where]
        else:
            dom = eng = where
        deps = []
        if dom in self.counts and dom not in self.ENGS:
            prev = self.last_dma.get(dom)
            if prev is not None:
                deps.append(prev)
        def _base_init(k):
            base = k[0] if (isinstance(k, tuple) and isinstance(k[0], tuple)) \
                else None
            if base is not None:
                eb = self.res.get(base)
                if eb:
                    deps.extend(eb.get("init", ()))

        for r in reads:
            e = self.res.get(r)
            if e:
                if e["w"] is not None:
                    deps.append(e["w"])
                deps.extend(e.get("init", ()))
            _base_init(r)
        for w in writes:
            e = self.res.get(w)
            if e:
                if e["w"] is not None:
                    deps.append(e["w"])
                deps.extend(e["r"])
                deps.extend(e.get("init", ()))
            _base_init(w)
        waits = []
        ec = self.eng_clock[eng]
        for (ddom, didx, dcvec) in deps:
            if ddom == "PE" and dom == "PE":
                continue
            if ec.get(ddom, 0) >= didx + 1:
                continue
            waits.append((ddom, didx))
            for d2, v in dcvec.items():
                if v > ec.get(d2, 0):
                    ec[d2] = v
            if ec.get(ddom, 0) < didx + 1:
                ec[ddom] = didx + 1
        idx = self.counts[dom]
        self.counts[dom] = idx + 1
        cvec = dict(ec)
        cvec[dom] = max(cvec.get(dom, 0), idx + 1)
        rec = _Op()
        rec.dom, rec.idx, rec.fn = dom, idx, fn
        rec.waits, rec.marked, rec.cvec, rec.tag = waits, False, cvec, tag
        self.ops[dom].append(rec)
        self.streams[eng].append(rec)
        ent = (dom, idx, cvec)
        if dom not in self.ENGS:
            self.last_dma[dom] = ent
        for r in reads:
            self._reg(r)
            self.res.setdefault(r, {"w": None, "r": []})["r"].append(ent)
        for w in writes:
            self._reg(w)
            e = self.res.setdefault(w, {"w": None, "r": []})
            e["w"] = ent
            e["r"] = []
            e.pop("init", None)
        for (ddom, didx) in waits:
            if ddom in self.ENGS:
                self.ops[ddom][didx].marked = True
        return rec

    def _reg(self, k):
        base = k[0] if (isinstance(k, tuple) and isinstance(k[0], tuple)) else k
        self.derived = getattr(self, "derived", {})
        self.derived.setdefault(base, set()).add(k)

    def seed_init(self, key, dep_ents):
        if dep_ents:
            self.res[key] = {"w": None, "r": [], "init": list(dep_ents)}

    def final_ents(self, key):
        self.derived = getattr(self, "derived", {})
        out = []
        for k in ({key} | self.derived.get(key, set())):
            e = self.res.get(k)
            if not e:
                continue
            if e["w"] is not None:
                out.append(e["w"])
            out.extend(e["r"])
            out.extend(e.get("init", ()))
        return out

    def finalize(self):
        waits = []
        for d in self.ENGS + self.qdoms:
            n = self.counts[d]
            if n > 0 and d != "SP":
                waits.append((d, n - 1))
                if d in self.ENGS:
                    self.ops[d][n - 1].marked = True
        rec = _Op()
        rec.dom, rec.idx, rec.fn = "SP", self.counts["SP"], (lambda e: e.nop())
        rec.waits, rec.marked, rec.cvec, rec.tag = waits, False, {}, "finalbar"
        self.counts["SP"] += 1
        self.ops["SP"].append(rec)
        self.streams["SP"].append(rec)
        self._pref = {}
        for d in self.ENGS:
            cnt, pref = 0, []
            for r in self.ops[d]:
                if r.marked:
                    cnt += 1
                pref.append(cnt)
            self._pref[d] = pref

    def emit(self, eng, engobj):
        pref = self._pref
        for rec in self.streams[eng]:
            for (ddom, didx) in rec.waits:
                val = pref[ddom][didx] if ddom in self.ENGS else 16 * (didx + 1)
                engobj.wait_ge(self.sems[ddom], val)
            inst = rec.fn(engobj)
            if rec.dom not in self.ENGS:
                inst.then_inc(self.sems[rec.dom], 16)
            elif rec.marked:
                inst.then_inc(self.sems[rec.dom], 1)


class SB:
    __slots__ = ("h", "key")

    def __init__(self, h, key):
        self.h = h
        self.key = key

    def __getitem__(self, sl):
        return self.h[sl]

    @property
    def ap(self):
        return self.h[:]


class _Row:
    __slots__ = ("b", "i", "pi", "key", "lkey")

    def __init__(self, b, i):
        self.b, self.i, self.pi = b, i, 64 * i
        self.key = (b.key, i)
        self.lkey = (b.key, i, "l")

    @property
    def ap(self):
        return self.b.h[self.pi:self.pi + 1, :]


class Arena:
    def __init__(self, nc, prog, base, top):
        self.nc, self.prog = nc, prog
        self.base, self.top = base, top
        self.cur = base
        self.live = {}
        self.dead = []
        self.marks = []
        self.uid = 0
        self.peak = base

    def alloc(self, name, shape, dtype):
        assert len(shape) == 2 and shape[0] <= 128
        bpp = shape[1] * DT_SIZE[dtype]
        self.uid += 1
        guard = self.nc.sbuf_tensor(f"{name}_{self.uid}", list(shape), dtype)
        h = guard.__enter__()
        mloc = self.nc.lookup_mloc(h)
        off = mloc.addr
        self.cur = max(self.cur, off + bpp)
        self.peak = max(self.peak, self.cur)
        key = (name, self.uid)
        init = []
        for i in range(len(self.dead) - 1, -1, -1):
            s, e, k2 = self.dead[i]
            if s < off + bpp and e > off:
                init.extend(self.prog.final_ents(k2))
        if init:
            self.prog.seed_init(key, init)
        self.live[key] = (off, off + bpp, guard)
        self.order = getattr(self, "order", [])
        self.order.append(key)
        return SB(h, key)

    def mark(self):
        self.marks.append(set(self.live))

    def release(self):
        keep = self.marks.pop()
        self.order = getattr(self, "order", [])
        for k in reversed(self.order):
            if k in self.live and k not in keep:
                s, e, guard = self.live[k]
                self.dead.append((s, e, k))
                del self.live[k]
                guard.__exit__(None, None, None)
        self.order = [k for k in self.order if k in self.live]


class Build:
    def __init__(self, dbg=(), phases=ALL_PHASES, moe_experts=None):
        self.dbg = set(dbg)
        self.phases = set(phases)
        self.moe_experts = list(range(1 + NE)) if moe_experts is None else list(moe_experts)
        self.nc = bass.Bass("TRN2", target_bir_lowering=False)
        self.p = Prog()
        self.consts = _host_consts()
        self.dbg_tensors = {}
        self._mkio()

    def _mkio(self):
        nc = self.nc
        self.xin = nc.dram_tensor("xin", [T, D], F32, kind="ExternalInput").ap()
        wshapes = {
            "w_qkv": [3 * D, D], "w_conv": [D, DH * CF], "b_conv": [D],
            "null_k": [D], "null_v": [D], "w_out": [D, D], "b_out": [D],
            "ln1_w": [D], "ln2_w": [D], "kv_d": [LAT, D], "q_d": [LAT, D],
            "k_u": [D, LAT], "q_u": [D, LAT], "v_u": [D, LAT],
            "rope_q": [NH * RD, LAT], "rope_k": [NH * RD, D],
            "o_w": [D, D], "router_w": [NE, D], "routing_bias": [NE],
            "sh_gate": [IE, D], "sh_up": [IE, D], "sh_down": [D, IE],
            "r_gate": [NE, IE, D], "r_up": [NE, IE, D], "r_down": [NE, D, IE],
        }
        self.win = {k: nc.dram_tensor(k, v, F32, kind="ExternalInput").ap()
                    for k, v in wshapes.items()}
        self.out = nc.dram_tensor("out", [T, D], F32, kind="ExternalOutput").ap()
        self.cst = {k: nc.inline_tensor(np.ascontiguousarray(self.consts[k]),
                                        name=k).ap()
                    for k in ("ident_bf", "ident_f32", "cosT", "sinT", "Rmat_bf",
                              "mask_diag_bf")}
        self.cst_om = {k: nc.inline_tensor(np.ascontiguousarray(v),
                                           name=f"om{k[0]}_{k[1]}").ap()
                       for k, v in self.consts["outer_masks"].items()}
        dram = lambda n, s, d: nc.dram_tensor(n, s, d, kind="Internal").ap()
        self.dr = {}
        for n, s, d in [
            ("wqkv_bf", [3 * D, D], BF16), ("wconv_bf", [D, DH * CF], BF16),
            ("wout_bf", [D, D], BF16), ("kvd_bf", [LAT, D], BF16),
            ("qd_bf", [LAT, D], BF16), ("ku_bf", [D, LAT], BF16),
            ("qu_bf", [D, LAT], BF16), ("vu_bf", [D, LAT], BF16),
            ("ropeq_bf", [NH * RD, LAT], BF16), ("ropek_bf", [NH * RD, D], BF16),
            ("ow_bf", [D, D], BF16), ("router_bf", [NE, D], BF16),
            ("sg_bf", [IE, D], BF16), ("su_bf", [IE, D], BF16),
            ("sd_bf", [D, IE], BF16), ("rg_bf", [NE, IE, D], BF16),
            ("ru_bf", [NE, IE, D], BF16), ("rd_bf", [NE, D, IE], BF16),
            ("vraw", [NCH, 128, T], BF16), ("qouter", [NCH, 128, T], BF16),
            ("cat_o", [NCH, 128, T], BF16), ("cat_i", [NCH, 128, T], BF16),
            ("x1", [T, D], F32), ("y0", [T, D], F32),
            ("kfull", [NCH, 128, T], BF16), ("qfull", [NCH, 128, T], BF16),
            ("vti", [T // 128, 128, NH * 65], BF16),
            ("xtf", [NCH, 128, T], BF16), ("wgt", [NE, T], F32),
            ("wgt_tm", [T // 128, 128, NE], F32),
        ]:
            self.dr[n] = dram(n, s, d)

    def dbg_out(self, name, shape, dtype):
        t = self.nc.dram_tensor(f"dbg_{name}", shape, dtype,
                                kind="ExternalOutput").ap()
        self.dbg_tensors[name] = t
        return t

    def dbg_dump_dram(self, name, src, reads):
        if name not in self.dbg:
            return
        t = self.dbg_out(name, list(src.shape), src.dtype)
        self.p.op("QSP", (lambda e: e.dma_start(out=t, in_=src)), reads=list(reads),
                  tag=f"dbg_{name}")

    def build(self):
        nc, p = self.nc, self.p
        with contextlib.ExitStack() as ctx:
            for d in Prog.ENGS + p.qdoms:
                p.sems[d] = ctx.enter_context(
                    nc.semaphore(f"s{d}".replace("#", "_")))
            self.pb = []
            for i in range(6):
                self.pb.append(ctx.enter_context(
                    nc.psum_tensor(f"pb{i}", [128, 512], F32)))
            for i in range(6, 8):
                self.pb.append(ctx.enter_context(
                    nc.psum_tensor(f"pb{i}", [128, 1024], BF16)))
            self.ar = Arena(nc, p, nc.sbuf_base, nc.sbuf_top)
            self._mk_small()
            self._stage_weights()
            ar = self.ar

            def _drive(gens):
                for g, _n, _l in gens:
                    next(g)
                maxlag = max(l for _g, _n, l in gens) if gens else 0
                for i in range(NTT + maxlag):
                    act = [j for j, (_g, _n, l) in enumerate(gens)
                           if 0 <= i - l < NTT]
                    cnt = {j: 0 for j in act}
                    while any(cnt[j] < gens[j][1] for j in act):
                        best, bfrac = -1, 2.0
                        for j in act:
                            if cnt[j] < gens[j][1]:
                                fr = cnt[j] / gens[j][1]
                                if fr < bfrac:
                                    best, bfrac = j, fr
                        next(gens[best][0])
                        cnt[best] += 1
                for g, _n, _l in gens:
                    for _ in g:
                        pass

            k_c = v_cT = None
            if "A" in self.phases or "A2" in self.phases or "B" in self.phases:
                k_c, v_cT = self._outer_attn_prep()
            if k_c is not None:
                self._emit_late_casts([("qouter",)])
            if "B" in self.phases or "CD" in self.phases:
                self._stage_moe_casts("shared", [("qouter",)])
                self._stage_moe_casts(0, [("qouter",)])
                self._stage_moe_casts(1, [("qouter",)])
                ar.mark()
                gens = []
                if "B" in self.phases:
                    gens.append((self.phase_B_iter(k_c, v_cT), NCH, 0))
                if "CD" in self.phases:
                    gens.append((self.phase_CD_iter(), 9, 1))
                _drive(gens)
                ar.release()
            if k_c is not None:
                ar.release()
            if "E" in self.phases or "F" in self.phases:
                for ee in (2, 3, 4, 5, 6):
                    self._stage_moe_casts(ee, [("x1",)])
                ar.mark()
                gens = []
                if "E" in self.phases:
                    gens.append((self.phase_E_iter(), NCH, 0))
                if "F" in self.phases:
                    gens.append((self.phase_F_iter(), 5, 1))
                _drive(gens)
                ar.release()
            if "G" in self.phases:
                self.phase_G()
            p.finalize()
            with nc.Block() as block:
                @block.tensor
                def _(eng):
                    p.emit("PE", eng)

                @block.vector
                def _(eng):
                    p.emit("DVE", eng)

                @block.scalar
                def _(eng):
                    p.emit("ACT", eng)

                @block.gpsimd
                def _(eng):
                    p.emit("POOL", eng)

                @block.sync
                def _(eng):
                    p.emit("SP", eng)
        return nc

    def _mk_small(self):
        p, ar = self.p, self.ar
        self.ident = ar.alloc("ident", [128, 128], BF16)
        self.ident32 = ar.alloc("ident32", [128, 128], F32)
        self.rmat = ar.alloc("rmat", [128, 128], BF16)
        self.mdiag = ar.alloc("mdiag", [128, 128], BF16)
        self.eps_t = ar.alloc("eps_t", [128, 1], F32)
        self.sc18 = ar.alloc("sc18", [128, 1], F32)
        self.sc132 = ar.alloc("sc132", [128, 1], F32)
        self.onesbf = ar.alloc("onesbf", [1, 128], BF16)
        self.onesk1 = ar.alloc("onesk1", [128, 128], F32)
        self.invD = ar.alloc("invD", [128, 1], F32)
        for dst, src in [(self.ident, self.cst["ident_bf"]),
                         (self.ident32, self.cst["ident_f32"]),
                         (self.rmat, self.cst["Rmat_bf"]),
                         (self.mdiag, self.cst["mask_diag_bf"])]:
            p.op("QSP", (lambda e, a=dst.ap, b=src: e.dma_start(out=a, in_=b)),
                 writes=[dst.key])
        for t, v in [(self.eps_t, EPS), (self.sc18, 0.125), (self.sc132, 1 / 32.0),
                     (self.onesbf, 1.0), (self.onesk1, 1.0),
                     (self.invD, 1.0 / D)]:
            p.op("DVE", (lambda e, a=t.ap, v=v: e.memset(a, v)), writes=[t.key])

    def _stage_weights(self):
        p, w, d = self.p, self.win, self.dr
        casts = [("wqkv_bf", w["w_qkv"]), ("wconv_bf", w["w_conv"]),
                 ("wout_bf", w["w_out"]), ("kvd_bf", w["kv_d"]),
                 ("qd_bf", w["q_d"]), ("ku_bf", w["k_u"]), ("qu_bf", w["q_u"]),
                 ("vu_bf", w["v_u"]), ("ropeq_bf", w["rope_q"]),
                 ("ropek_bf", w["rope_k"]), ("ow_bf", w["o_w"]),
                 ("router_bf", w["router_w"]), ("sg_bf", w["sh_gate"]),
                 ("su_bf", w["sh_up"]), ("sd_bf", w["sh_down"])]
        need = {"A": ["wqkv_bf", "wconv_bf"], "A2": ["wconv_bf"],
                "CD": ["wout_bf", "kvd_bf", "qd_bf", "ku_bf", "qu_bf", "vu_bf",
                       "ropeq_bf", "ropek_bf"],
                "F": ["ow_bf", "router_bf"],
                "G": ["sg_bf", "su_bf", "sd_bf", "rg_bf", "ru_bf", "rd_bf"]}
        wanted = set()
        for ph in self.phases:
            wanted.update(need.get(ph, ()))
        self._late_casts = []
        for nm, src in casts:
            if nm not in wanted or nm in ("sg_bf", "su_bf", "sd_bf"):
                continue
            if nm in ("wqkv_bf", "wconv_bf"):
                p.op("QPOOL", (lambda e, a=d[nm], b=src: e.dma_start(out=a,
                                                                    in_=b)),
                     writes=[("cast", nm)], tag=f"cast_{nm}")
            else:
                self._late_casts.append((nm, src))

    def _emit_late_casts(self, gate_reads):
        p, d = self.p, self.dr
        for nm, src in getattr(self, "_late_casts", []):
            p.op("QPOOL", (lambda e, a=d[nm], b=src: e.dma_start(out=a, in_=b)),
                 reads=list(gate_reads), writes=[("cast", nm)],
                 tag=f"cast_{nm}")
        self._late_casts = []

    def _stage_moe_casts(self, group, gate_reads):
        p, w, d = self.p, self.win, self.dr
        if "G" not in self.phases:
            return
        if group == "shared":
            for nm, src in [("sg_bf", w["sh_gate"]), ("su_bf", w["sh_up"]),
                            ("sd_bf", w["sh_down"])]:
                p.op("QPOOL", (lambda e, a=d[nm], b=src: e.dma_start(out=a,
                                                                    in_=b)),
                     reads=list(gate_reads), writes=[("cast", nm)],
                     tag=f"cast_{nm}")
        else:
            ee = group
            for nm, src in [("rg_bf", w["r_gate"]), ("ru_bf", w["r_up"]),
                            ("rd_bf", w["r_down"])]:
                p.op("QPOOL", (lambda e, a=d[nm][ee], b=src[ee]:
                               e.dma_start(out=a, in_=b)),
                     reads=list(gate_reads), writes=[("cast", nm, ee)],
                     tag=f"cast_{nm}{ee}")

    def wload_T(self, sb, src2d, castkey, col0=0, ncols=None, queue="QACT"):
        dst = sb.h[0:(src2d.shape[1]), col0:col0 + src2d.shape[0]] \
            if False else sb.h[:, col0:col0 + src2d.shape[0]]
        self.p.op(queue, (lambda e, a=dst, b=src2d: e.dma_start_transpose(a, b)),
                  reads=[castkey], writes=[(sb.key, "ld", col0)], tag="wT")

    def _ln_tok(self, xb, xkey, lnb, lnkey, sc, wmul=None, tag="ln"):
        p = self.p
        sq, s1, s2, mean, m2t, var, rstd = sc
        p.op("DVE", (lambda e: e.scalar_tensor_tensor(
            sq.ap, xb, 1.0, xb, ALU.mult, ALU.mult, accum_out=s2.ap)),
             reads=[xkey], writes=[sq.key, s2.key], tag=tag + "sq")
        p.op("DVE", (lambda e: e.tensor_scalar(
            sq.ap, xb, 1.0, 0.0, ALU.mult, ALU.add, accum_out=s1.ap)),
             reads=[xkey], writes=[sq.key, s1.key], tag=tag + "s1")
        p.op("DVE", (lambda e: e.tensor_scalar_mul(mean.ap, s1.ap, 1.0 / D)),
             reads=[s1.key], writes=[mean.key])
        p.op("DVE", (lambda e: e.scalar_tensor_tensor(
            var.ap, s1.ap, -1.0 / D, s1.ap, ALU.mult, ALU.mult)),
             reads=[s1.key], writes=[(var.key, "t")], tag=tag + "t")
        p.op("DVE", (lambda e: e.tensor_add(var.ap, var.ap, s2.ap)),
             reads=[(var.key, "t"), s2.key], writes=[var.key], tag=tag + "u")
        p.op("ACT", (lambda e: e.activation(rstd.ap, var.ap, AF.Sqrt,
                                            bias=self.eps_t.ap,
                                            scale=self.invD.ap)),
             reads=[var.key, self.eps_t.key, self.invD.key],
             writes=[(rstd.key, "s")])
        p.op("DVE", (lambda e: e.reciprocal(rstd.ap, rstd.ap)),
             reads=[(rstd.key, "s")], writes=[rstd.key])
        if wmul is None:
            p.op("DVE", (lambda e: e.tensor_scalar(
                lnb, xb, mean.ap, rstd.ap, ALU.subtract, ALU.mult)),
                 reads=[xkey, mean.key, rstd.key], writes=[lnkey], tag=tag + "ap")
        else:
            tmp, tkey = wmul[1], wmul[1].key
            p.op("DVE", (lambda e: e.tensor_scalar(
                tmp.ap, xb, mean.ap, rstd.ap, ALU.subtract, ALU.mult)),
                 reads=[xkey, mean.key, rstd.key], writes=[tkey], tag=tag + "ap")
            p.op("DVE", (lambda e: e.tensor_mul(lnb, tmp.ap, wmul[0].ap)),
                 reads=[tkey, wmul[0].key], writes=[lnkey], tag=tag + "wm")

    def _mk_ln_scratch(self, ar, pref, sqbuf=None):
        return (sqbuf if sqbuf is not None else ar.alloc(pref + "sq", [128, D], F32),
                ar.alloc(pref + "s1", [128, 1], F32),
                ar.alloc(pref + "s2", [128, 1], F32),
                ar.alloc(pref + "mean", [128, 1], F32),
                ar.alloc(pref + "m2t", [128, 1], F32),
                ar.alloc(pref + "var", [128, 1], F32),
                ar.alloc(pref + "rstd", [128, 1], F32))

    def _transpose_tok_to_f(self, lnb, lnkey, dsts, dcol, tag="T"):
        p = self.p
        for half in range(2):
            bank = 6 + half
            for cc in range(4):
                c = half * 4 + cc
                p.op("PE", (lambda e, b=self.pb[bank], cc=cc, c=c:
                            e.transpose(b[:, cc * 128:(cc + 1) * 128],
                                        lnb[:, c * 128:(c + 1) * 128],
                                        self.ident.ap)),
                     reads=[lnkey, self.ident.key], writes=[f"PS{bank}"],
                     tag=tag + "t")
            eng = "DVE" if half == 0 else "ACT"
            for cc in range(4):
                c = half * 4 + cc
                p.op(eng, (lambda e, b=self.pb[bank], cc=cc, c=c:
                           (e.tensor_copy if hasattr(e, "tensor_copy") else e.copy)(
                               dsts[c].h[:, dcol:dcol + 128],
                               b[:, cc * 128:(cc + 1) * 128])),
                     reads=[f"PS{bank}"], writes=[(dsts[c].key, "q", dcol)],
                     tag=tag + "e")

    def _outer_attn_prep(self):
        nc, p, ar = self.nc, self.p, self.ar
        ar.mark()
        k_c = [ar.alloc(f"kc{c}", [128, JTOT], BF16) for c in range(NCH)]
        v_cT = [ar.alloc(f"vcT{j}", [128, NH * 65], BF16) for j in range(NJT)]
        ar.mark()
        wcT = [ar.alloc(f"wcT{t}", [128, D], BF16) for t in range(CF)]
        bconv = ar.alloc("bconv", [128, NCH], F32)
        bconv_row = ar.alloc("bconvrow", [1, D], F32)
        ckc = ("cast", "wconv_bf")
        wcnat = ar.alloc("wcnat", [128, DH * CF], BF16)
        for c in range(NCH):
            p.op("QACT", (lambda e, a=wcnat.ap,
                          b=self.dr["wconv_bf"][c * 128:(c + 1) * 128, :]:
                          e.dma_start(out=a, in_=b)),
                 reads=[ckc], writes=[wcnat.key], tag="wcnat")
            for t in range(CF):
                bank = 6 + (c * CF + t) % 2
                p.op("PE", (lambda e, b=self.pb[bank], t=t:
                            e.transpose(b[0:64, 0:128],
                                        wcnat.h[:, t:DH * CF:CF], self.ident.ap)),
                     reads=[wcnat.key, self.ident.key], writes=[f"PS{bank}"])
                p.op("DVE", (lambda e, b=self.pb[bank], t=t, c=c:
                             e.tensor_copy(wcT[t].h[0:64, c * 128:(c + 1) * 128],
                                           b[0:64, 0:128])),
                     reads=[f"PS{bank}"], writes=[wcT[t].key])
                p.op("DVE", (lambda e, b=self.pb[bank], t=t, c=c:
                             e.tensor_copy(wcT[t].h[64:128, c * 128:(c + 1) * 128],
                                           b[0:64, 0:128])),
                     reads=[f"PS{bank}"], writes=[wcT[t].key])
        p.op("QSP", (lambda e: e.dma_start(
            out=bconv.ap, in_=self.win["b_conv"].rearrange("(c p) -> p c", p=128),
            allow_slow_non_contiguous=True)), writes=[bconv.key])
        p.op("QSP", (lambda e: e.dma_start(
            out=bconv_row.ap, in_=self.win["b_conv"].rearrange("(a d) -> a d", a=1))),
             writes=[bconv_row.key])
        for c in range(NCH):
            p.op("QPOOL", (lambda e, c=c: e.dma_start(
                out=k_c[c].h[:, JTOT - 1:JTOT],
                in_=self.win["null_k"][c * 128:(c + 1) * 128].rearrange("(p a) -> p a", a=1),
                allow_slow_non_contiguous=True)),
                 writes=[(k_c[c].key, "null")], tag="nullk")
        for j in range(NJT):
            p.op("DVE", (lambda e, j=j: e.memset(v_cT[j].ap, 0.0)),
                 writes=[(v_cT[j].key, "zero")])
            p.op("DVE", (lambda e, j=j: e.memset(
                v_cT[j].h[:, 64:NH * 65:65], 1.0)),
                 reads=[(v_cT[j].key, "zero")], writes=[(v_cT[j].key, "ones")])
        p.op("QPOOL", (lambda e: e.dma_start(
            out=v_cT[10].h[86:87, :].rearrange("p (h x) -> p h x", x=65)[:, :, 0:64],
            in_=self.win["null_v"].rearrange("(a h x) -> a h x", a=1, x=64),
            allow_slow_non_contiguous=True)),
             reads=[(v_cT[10].key, "zero"), (v_cT[10].key, "ones")],
             writes=[(v_cT[10].key, "null")], tag="nullv")

        if "A" in self.phases:
            self.phase_A(k_c, wcT, bconv)
        if "A2" in self.phases:
            self.phase_A2(v_cT, wcT, bconv_row)
        ar.release()
        return k_c, v_cT

    def phase_A(self, k_c, wcT, bconv):
        nc, p, ar = self.nc, self.p, self.ar
        ar.mark()
        wqkvT = [ar.alloc(f"wqkvT{c}", [128, 3 * D], BF16) for c in range(NCH)]
        ck = ("cast", "wqkv_bf")
        for c in range(NCH):
            p.op("QACT", (lambda e, a=wqkvT[c].ap,
                          b=self.dr["wqkv_bf"][:, c * 128:(c + 1) * 128]:
                          e.dma_start_transpose(a, b)),
                 reads=[ck], writes=[wqkvT[c].key], tag="wqkvT")
        x_tok = [ar.alloc(f"xtok{i}", [128, D], F32) for i in range(4)]
        lnsc2 = [self._mk_ln_scratch(ar, f"a{i}") for i in range(2)]
        ln_tok = [ar.alloc(f"lntok{i}", [128, D], BF16) for i in range(4)]
        ln_f2 = [[ar.alloc(f"lnf{pp}_{c}", [128, TT], BF16) for c in range(NCH)]
                 for pp in range(2)]
        kraw = [ar.alloc(f"kraw{c}", [128, 2 + TT], BF16) for c in range(NCH)]
        vraw_t = [ar.alloc(f"vrawt{c}", [128, TT], BF16) for c in range(NCH)]
        qtile = [ar.alloc(f"qt{c}", [128, TT], BF16) for c in range(NCH)]
        for c in range(NCH):
            p.op("DVE", (lambda e, c=c: e.memset(kraw[c].h[:, 0:2], 0.0)),
                 writes=[(kraw[c].key, "hdr")], tag="kpad0")

        def _qkv_conv(tt):
            ln_f = ln_f2[tt % 2]
            lnf_reads = [(ln_f[c].key, "q", s * 128)
                         for c in range(NCH) for s in range(4)]
            for m in range(24):
                bank = m % 4
                for k in range(NCH):
                    p.op("PE", (lambda e, b=self.pb[bank], k=k, m=m,
                                w=wqkvT[k], lf=ln_f[k]:
                                e.matmul(b[:, 0:TT],
                                         w.h[:, m * 128:(m + 1) * 128],
                                         lf.ap, start=(k == 0),
                                         stop=(k == NCH - 1))),
                         reads=[wqkvT[k].key] + lnf_reads, writes=[f"PS{bank}"],
                         tag="qkv")
                if m < 8:
                    dst, dkey = qtile[m].ap, qtile[m].key
                elif m < 16:
                    dst, dkey = kraw[m - 8].h[:, 2:2 + TT], (kraw[m - 8].key, "body")
                else:
                    dst, dkey = vraw_t[m - 16].ap, vraw_t[m - 16].key
                p.op("ACT", (lambda e, b=self.pb[bank], d=dst:
                             e.copy(d, b[:, 0:TT])),
                     reads=[f"PS{bank}"], writes=[dkey], tag="qkv_ev")
            for c in range(NCH):
                p.op("QACT", (lambda e, c=c, tt=tt: e.dma_start(
                    out=self.dr["qouter"][c, :, tt * TT:(tt + 1) * TT],
                    in_=qtile[c].ap)),
                     reads=[qtile[c].key], writes=[("qouter",)], tag="qspill")
                p.op("QACT", (lambda e, c=c, tt=tt: e.dma_start(
                    out=self.dr["vraw"][c, :, tt * TT:(tt + 1) * TT],
                    in_=vraw_t[c].ap)),
                     reads=[vraw_t[c].key], writes=[("vraw",)], tag="vspill")
            j0 = -(-512 * tt // 3)
            j1 = (512 * tt + 511) // 3
            nj = j1 - j0 + 1
            for g in range(NH):
                c, hh = g // 2, (g % 2) * 64
                bank = 4 + g % 2
                for t in range(CF):
                    off = 3 * j0 + t - 512 * tt
                    p.op("PE", (lambda e, b=self.pb[bank], c=c, hh=hh, t=t,
                                off=off, nj=nj, g=g:
                                e.matmul(b[0:64, 0:nj],
                                         wcT[t].h[hh:hh + 64, g * 64:g * 64 + 64],
                                         kraw[c].h[hh:hh + 64, off:off + 3 * (nj - 1) + 1:3],
                                         start=(t == 0), stop=(t == CF - 1))),
                         reads=[wcT[t].key, (kraw[c].key, "body"),
                                (kraw[c].key, "hdr")],
                         writes=[f"PS{bank}"], tag="kconv")
                p.op("DVE", (lambda e, b=self.pb[bank], c=c, hh=hh, j0=j0, nj=nj:
                             e.tensor_scalar_add(k_c[c].h[hh:hh + 64, j0:j0 + nj],
                                                 b[0:64, 0:nj],
                                                 bconv.h[hh:hh + 64, c:c + 1])),
                     reads=[f"PS{bank}", bconv.key], writes=[(k_c[c].key, "cv")],
                     tag="kconv_ev")
            if tt < NTT - 1:
                for c in range(NCH):
                    p.op("DVE", (lambda e, c=c: e.tensor_copy(
                        kraw[c].h[:, 0:2], kraw[c].h[:, TT:TT + 2])),
                         reads=[(kraw[c].key, "body")],
                         writes=[(kraw[c].key, "hdr")], tag="kroll")

        for tt in range(NTT):
            for s in range(4):
                xb, lnb = x_tok[s], ln_tok[s]
                t0 = tt * TT + s * 128
                p.op("QSP", (lambda e, a=xb.ap, b=self.xin[t0:t0 + 128, :]:
                             e.dma_start(out=a, in_=b)),
                     writes=[xb.key], tag="xload")
                self._ln_tok(xb.ap, xb.key, lnb.ap, lnb.key, lnsc2[s % 2],
                             tag="ln0")
            if tt > 0:
                _qkv_conv(tt - 1)
            for s in range(4):
                self._transpose_tok_to_f(ln_tok[s], ln_tok[s].key,
                                         ln_f2[tt % 2], s * 128, tag="ln0T")
        _qkv_conv(NTT - 1)
        if "k_c" in self.dbg:
            t = self.dbg_out("k_c", [NCH, 128, JTOT], BF16)
            for c in range(NCH):
                p.op("QSP", (lambda e, c=c, t=t: e.dma_start(out=t[c],
                                                             in_=k_c[c].ap)),
                     reads=[(k_c[c].key, "cv"), (k_c[c].key, "null")], tag="dbgkc")
        self.dbg_dump_dram("q_f", self.dr["qouter"], [("qouter",)])
        self.dbg_dump_dram("vraw", self.dr["vraw"], [("vraw",)])
        ar.release()

    def phase_A2(self, v_cT, wcT, bconv_row):
        nc, p, ar = self.nc, self.p, self.ar
        ar.mark()
        vpad = [ar.alloc(f"vpad{c}", [128, 392], BF16) for c in range(NCH)]
        for jt in range(NJT):
            j_hi = min(jt * 128 + 127, NCONV - 1)
            njj = j_hi - jt * 128 + 1
            lo_pad = 384 * jt
            ln = 3 * j_hi + 2 - lo_pad + 1
            for c in range(NCH):
                if jt == 0:
                    p.op("DVE", (lambda e, c=c: e.memset(vpad[c].h[:, 0:2], 0.0)),
                         writes=[(vpad[c].key, "h")])
                    p.op("QSP", (lambda e, c=c, ln=ln: e.dma_start(
                        out=vpad[c].h[:, 2:ln],
                        in_=self.dr["vraw"][c, :, 0:ln - 2])),
                         reads=[("vraw",)], writes=[vpad[c].key], tag="vpadld")
                else:
                    p.op("QSP", (lambda e, c=c, lo=lo_pad - 2, ln=ln: e.dma_start(
                        out=vpad[c].h[:, 0:ln],
                        in_=self.dr["vraw"][c, :, lo:lo + ln])),
                         reads=[("vraw",)], writes=[vpad[c].key], tag="vpadld")
            for g in range(NH):
                c, hh = g // 2, (g % 2) * 64
                bank = g % 2
                for t in range(CF):
                    p.op("PE", (lambda e, b=self.pb[bank], c=c, hh=hh, t=t,
                                njj=njj, g=g:
                                e.matmul(b[0:njj, 0:64],
                                         vpad[c].h[hh:hh + 64, t:t + 3 * (njj - 1) + 1:3],
                                         wcT[t].h[hh:hh + 64, g * 64:g * 64 + 64],
                                         start=(t == 0), stop=False)),
                         reads=[wcT[t].key, vpad[c].key, (vpad[c].key, "h")],
                         writes=[f"PS{bank}"], tag="vconv")
                p.op("PE", (lambda e, b=self.pb[bank], njj=njj, g=g:
                            e.matmul(b[0:njj, 0:64],
                                     self.onesk1.h[0:1, 0:njj],
                                     bconv_row.h[0:1, g * 64:g * 64 + 64],
                                     start=False, stop=True)),
                     reads=[self.onesk1.key, bconv_row.key],
                     writes=[f"PS{bank}"], tag="vconv_b")
                p.op("DVE", (lambda e, b=self.pb[bank], jt=jt, g=g, njj=njj:
                             e.tensor_copy(
                                 v_cT[jt].h[0:njj, g * 65:g * 65 + 64],
                                 b[0:njj, 0:64])),
                     reads=[f"PS{bank}", (v_cT[jt].key, "zero"),
                            (v_cT[jt].key, "ones")],
                     writes=[(v_cT[jt].key, "cv")], tag="vconv_ev")
        if "v_cT" in self.dbg:
            t = self.dbg_out("v_cT", [NJT, 128, NH * 65], BF16)
            for j in range(NJT):
                p.op("QSP", (lambda e, j=j, t=t: e.dma_start(out=t[j],
                                                             in_=v_cT[j].ap)),
                     reads=[(v_cT[j].key, "cv"), (v_cT[j].key, "ones"),
                            (v_cT[j].key, "null")], tag="dbgvc")
        ar.release()

    def phase_B_iter(self, k_c, v_cT):
        nc, p, ar = self.nc, self.p, self.ar
        kind = self.consts["outer_kind"]
        mbufs = [ar.alloc(f"om{i}", [128, TT], BF16) for i in range(3)]
        masks = {}

        def _load_masks(qg):
            need = [jt for jt in range(NJT) if kind[jt, qg] == 2]
            masks.clear()
            assert len(need) <= 3
            for i, jt in enumerate(need):
                m = mbufs[i % 3]
                p.op("QSP", (lambda e, a=m.ap, b=self.cst_om[(jt, qg)]:
                             e.dma_start(out=a, in_=b)),
                     writes=[m.key], tag="maskld")
                masks[(jt, qg)] = m
        qch = [ar.alloc(f"bq{i}", [128, TT], BF16) for i in range(2)]
        ptile = [ar.alloc(f"bp{i}", [128, TT], BF16) for i in range(6)]
        catb = [ar.alloc(f"bcat{i}", [64, TT], BF16) for i in range(4)]
        lrow4 = [ar.alloc(f"blr4{j}", [128, TT], F32) for j in range(2)]
        lrow = [_Row(lrow4[i // 2], i % 2) for i in range(4)]
        pend_tail = []
        yield
        for qg in range(NTT):
            _load_masks(qg)
            for c in range(NCH):
                qb = qch[c % 2]
                p.op("QSP", (lambda e, a=qb.ap, c=c, qg=qg: e.dma_start(
                    out=a, in_=self.dr["qouter"][c, :, qg * TT:(qg + 1) * TT])),
                     reads=[("qouter",)], writes=[qb.key], tag="bqld")
                jts = [jt for jt in range(NJT) if kind[jt, qg] != 0]

                def _qk(ji, jt, c=c, qb=qb, qg=qg):
                    mj = min(128, JTOT - jt * 128)
                    for half in range(2):
                        sb = 2 * (ji % 2) + half
                        hh = half * 64
                        pt = ptile[2 * (ji % 3) + half]
                        p.op("PE", (lambda e, b=self.pb[sb], c=c, hh=hh, jt=jt,
                                    mj=mj, qb=qb:
                                    e.matmul(b[0:mj, 0:TT],
                                             k_c[c].h[hh:hh + 64,
                                                      jt * 128:jt * 128 + mj],
                                             qb.h[hh:hh + 64, :],
                                             start=True, stop=True)),
                             reads=[(k_c[c].key, "cv"), (k_c[c].key, "null"),
                                    qb.key],
                             writes=[f"PS{sb}"], tag="oqk")
                        p.op("ACT", (lambda e, b=self.pb[sb], pt=pt, mj=mj:
                                     e.activation(pt.h[0:mj, :], b[0:mj, 0:TT],
                                                  AF.Exp,
                                                  scale=self.sc132.h[0:mj, :])),
                             reads=[f"PS{sb}", self.sc132.key], writes=[pt.key],
                             tag="oexp")
                        if kind[jt, qg] == 2:
                            mk = masks[(jt, qg)]
                            p.op("DVE", (lambda e, pt=pt, mk=mk, mj=mj:
                                         e.tensor_mul(pt.h[0:mj, :],
                                                      pt.h[0:mj, :],
                                                      mk.h[0:mj, :])),
                                 reads=[mk.key, pt.key], writes=[pt.key],
                                 tag="omask")

                def _av(ji, jt, c=c):
                    mj = min(128, JTOT - jt * 128)
                    for half in range(2):
                        ab = 4 + half
                        h = 2 * c + half
                        pt = ptile[2 * (ji % 3) + half]
                        p.op("PE", (lambda e, b=self.pb[ab], jt=jt, h=h, mj=mj,
                                    pt=pt, first=(ji == 0),
                                    last=(jt == jts[-1]):
                                    e.matmul(b[0:65, 0:TT],
                                             v_cT[jt].h[0:mj, h * 65:h * 65 + 65],
                                             pt.h[0:mj, :],
                                             start=first, stop=last)),
                             reads=[(v_cT[jt].key, "cv"), (v_cT[jt].key, "ones"),
                                    (v_cT[jt].key, "null"), pt.key],
                             writes=[f"PS{ab}"], tag="oav")

                for ji in range(len(jts) + 2):
                    if ji < len(jts):
                        _qk(ji, jts[ji])
                    if ji == 1 and pend_tail:
                        for f in pend_tail:
                            f()
                        pend_tail = []
                    if ji >= 2:
                        _av(ji - 2, jts[ji - 2])
                for half in range(2):
                    ab = 4 + half
                    spb = 2 * (len(jts) % 2) + half
                    hh = half * 64
                    cb = catb[2 * (c % 2) + half]
                    lr = lrow[2 * (c % 2) + half]
                    p.op("ACT", (lambda e, b=self.pb[ab], cb=cb:
                                 e.copy(cb.ap, b[0:64, 0:TT])),
                         reads=[f"PS{ab}"], writes=[(cb.key, "raw")], tag="ocat")
                    p.op("DVE", (lambda e, b=self.pb[ab], lr=lr:
                                 e.tensor_copy(lr.ap, b[64:65, 0:TT])),
                         reads=[f"PS{ab}"], writes=[lr.lkey], tag="olcp")
                    p.op("DVE", (lambda e, lr=lr: e.reciprocal(lr.ap, lr.ap)),
                         reads=[lr.lkey], writes=[lr.key], tag="olinv")

                    def _tail(spb=spb, cb=cb, lr=lr, c=c, hh=hh, qg=qg):
                        p.op("PE", (lambda e, b=self.pb[spb], lr=lr:
                                    e.matmul(b[0:64, 0:TT],
                                             self.onesk1.h[lr.pi:lr.pi + 1,
                                                           0:64],
                                             lr.ap, start=True, stop=True)),
                             reads=[self.onesk1.key, lr.key],
                             writes=[f"PS{spb}"], tag="obc")
                        p.op("DVE", (lambda e, b=self.pb[spb], cb=cb:
                                     e.tensor_mul(cb.ap, cb.ap, b[0:64, 0:TT])),
                             reads=[f"PS{spb}", (cb.key, "raw")],
                             writes=[cb.key], tag="onorm")
                        p.op("QACT", (lambda e, cb=cb, c=c, hh=hh, qg=qg:
                                      e.dma_start(
                                          out=self.dr["cat_o"][c, hh:hh + 64,
                                                               qg * TT:
                                                               (qg + 1) * TT],
                                          in_=cb.ap)),
                             reads=[cb.key], writes=[("cat_o",)], tag="ocatst")
                    pend_tail.append(_tail)
                if c == NCH - 1:
                    for f in pend_tail:
                        f()
                    pend_tail = []
                yield
        self.dbg_dump_dram("cat_o", self.dr["cat_o"], [("cat_o",)])

    def phase_CD_iter(self):
        nc, p, ar = self.nc, self.p, self.ar
        w_outT = [ar.alloc(f"woT{k}", [128, D], BF16) for k in range(NCH)]
        for k in range(NCH):
            p.op("QACT", (lambda e, a=w_outT[k].ap,
                          b=self.dr["wout_bf"][:, k * 128:(k + 1) * 128]:
                          e.dma_start_transpose(a, b)),
                 reads=[("cast", "wout_bf")], writes=[w_outT[k].key], tag="woT")
        kv_dT = [ar.alloc(f"kvdT{k}", [128, LAT], BF16) for k in range(NCH)]
        q_dT = [ar.alloc(f"qdT{k}", [128, LAT], BF16) for k in range(NCH)]
        for k in range(NCH):
            p.op("QACT", (lambda e, a=kv_dT[k].ap,
                          b=self.dr["kvd_bf"][:, k * 128:(k + 1) * 128]:
                          e.dma_start_transpose(a, b)),
                 reads=[("cast", "kvd_bf")], writes=[kv_dT[k].key])
            p.op("QACT", (lambda e, a=q_dT[k].ap,
                          b=self.dr["qd_bf"][:, k * 128:(k + 1) * 128]:
                          e.dma_start_transpose(a, b)),
                 reads=[("cast", "qd_bf")], writes=[q_dT[k].key])
        k_uT = [ar.alloc(f"kuT{k}", [128, D], BF16) for k in range(2)]
        q_uT = [ar.alloc(f"quT{k}", [128, D], BF16) for k in range(2)]
        v_uT = [ar.alloc(f"vuT{k}", [128, D], BF16) for k in range(2)]
        rope_qT = [ar.alloc(f"rqT{k}", [128, NH * RD], BF16) for k in range(2)]
        rope_kT = [ar.alloc(f"rkT{k}", [128, NH * RD], BF16) for k in range(NCH)]
        for k in range(2):
            for sb, nm in [(k_uT[k], "ku_bf"), (q_uT[k], "qu_bf"),
                           (v_uT[k], "vu_bf")]:
                p.op("QACT", (lambda e, a=sb.ap,
                              b=self.dr[nm][:, k * 128:(k + 1) * 128]:
                              e.dma_start_transpose(a, b)),
                     reads=[("cast", nm)], writes=[sb.key])
            p.op("QACT", (lambda e, a=rope_qT[k].ap,
                          b=self.dr["ropeq_bf"][:, k * 128:(k + 1) * 128]:
                          e.dma_start_transpose(a, b)),
                 reads=[("cast", "ropeq_bf")], writes=[rope_qT[k].key])
        for k in range(NCH):
            p.op("QACT", (lambda e, a=rope_kT[k].ap,
                          b=self.dr["ropek_bf"][:, k * 128:(k + 1) * 128]:
                          e.dma_start_transpose(a, b)),
                 reads=[("cast", "ropek_bf")], writes=[rope_kT[k].key])
        KUSEL = [[ar.alloc(f"kus{c}_{k}", [128, 128], BF16) for k in range(2)]
                 for c in range(NCH)]
        RKSEL = [[ar.alloc(f"rks{c}_{k}", [128, 128], BF16) for k in range(NCH)]
                 for c in range(NCH)]
        QCOMB = [ar.alloc(f"qcmb{k}", [128, D], BF16) for k in range(2)]
        for c in range(NCH):
            for k in range(2):
                sbb = KUSEL[c][k]
                p.op("DVE", (lambda e, a=sbb.ap: e.memset(a, 0.0)),
                     writes=[(sbb.key, "z")])
                for hi in range(2):
                    p.op("DVE", (lambda e, sbb=sbb, hi=hi, c=c, k=k:
                                 e.tensor_copy(
                                     sbb.h[:, 64 * hi + 32:64 * hi + 64],
                                     k_uT[k].h[:, (2 * c + hi) * 64 + 32:
                                               (2 * c + hi) * 64 + 64])),
                         reads=[(sbb.key, "z"), k_uT[k].key], writes=[sbb.key])
            for k in range(NCH):
                sbb = RKSEL[c][k]
                p.op("DVE", (lambda e, a=sbb.ap: e.memset(a, 0.0)),
                     writes=[(sbb.key, "z")])
                for hi in range(2):
                    p.op("DVE", (lambda e, sbb=sbb, hi=hi, c=c, k=k:
                                 e.tensor_copy(
                                     sbb.h[:, 64 * hi:64 * hi + 32],
                                     rope_kT[k].h[:, (2 * c + hi) * 32:
                                                  (2 * c + hi) * 32 + 32])),
                         reads=[(sbb.key, "z"), rope_kT[k].key],
                         writes=[sbb.key])
        for k in range(2):
            sbb = QCOMB[k]
            for c in range(NCH):
                for hi in range(2):
                    p.op("DVE", (lambda e, sbb=sbb, hi=hi, c=c, k=k:
                                 e.tensor_copy(
                                     sbb.h[:, c * 128 + 64 * hi:
                                           c * 128 + 64 * hi + 32],
                                     rope_qT[k].h[:, (2 * c + hi) * 32:
                                                  (2 * c + hi) * 32 + 32])),
                         reads=[rope_qT[k].key], writes=[sbb.key])
                    p.op("DVE", (lambda e, sbb=sbb, hi=hi, c=c, k=k:
                                 e.tensor_copy(
                                     sbb.h[:, c * 128 + 64 * hi + 32:
                                           c * 128 + 64 * hi + 64],
                                     q_uT[k].h[:, (2 * c + hi) * 64 + 32:
                                               (2 * c + hi) * 64 + 64])),
                         reads=[q_uT[k].key], writes=[sbb.key])
        ln1row = ar.alloc("ln1row", [1, D], F32)
        boutrow = ar.alloc("boutrow", [1, D], F32)
        p.op("QSP", (lambda e: e.dma_start(
            out=ln1row.ap, in_=self.win["ln1_w"].rearrange("(a d) -> a d", a=1))),
             writes=[ln1row.key])
        p.op("QSP", (lambda e: e.dma_start(
            out=boutrow.ap, in_=self.win["b_out"].rearrange("(a d) -> a d", a=1))),
             writes=[boutrow.key])
        ln1bc = ar.alloc("ln1bc", [128, D], F32)
        for half in range(2):
            p.op("PE", (lambda e, b=self.pb[5], half=half:
                        e.matmul(b[:, 0:TT], self.onesk1.h[0:1, 0:128],
                                 ln1row.h[0:1, half * 512:(half + 1) * 512],
                                 start=True, stop=True)),
                 reads=[self.onesk1.key, ln1row.key], writes=["PS5"], tag="ln1bc")
            p.op("DVE", (lambda e, b=self.pb[5], half=half:
                         e.tensor_copy(ln1bc.h[:, half * 512:(half + 1) * 512],
                                       b[:, 0:TT])),
                 reads=["PS5"], writes=[ln1bc.key])
        cost = ar.alloc("cost", [128, TT], F32)
        sint = ar.alloc("sint", [128, TT], F32)
        catk = [ar.alloc(f"cdcat{k}", [128, TT], BF16) for k in range(NCH)]
        xa2 = [ar.alloc("cdx", [128, D], F32)] * 2
        x1t2 = [ar.alloc("cdx1", [128, D], F32)] * 2
        y0t2 = [ar.alloc("cdy0", [128, D], F32)] * 2
        lnt2 = [ar.alloc("cdlnt", [128, D], F32)] * 2
        lnab2 = [ar.alloc(f"cdlna{i}", [128, D], BF16) for i in range(2)]
        lnsc2 = [self._mk_ln_scratch(ar, "cd")] * 2
        lnaf = [ar.alloc(f"cdlnaf{k}", [128, TT], BF16) for k in range(NCH)]
        kvlat = [ar.alloc(f"cdkvl{k}", [128, TT], BF16) for k in range(2)]
        qlat = [ar.alloc(f"cdql{k}", [128, TT], BF16) for k in range(2)]
        vts = [ar.alloc(f"cdvts{i}", [128, NH * 65], BF16) for i in range(2)]
        for i in range(2):
            p.op("DVE", (lambda e, i=i: e.memset(
                vts[i].h[:, 64:NH * 65:65], 1.0)), writes=[(vts[i].key, "ones")])
        pre2 = [ar.alloc(f"cdpre{i}", [128, TT], BF16) for i in range(2)]
        rt12 = [ar.alloc("cdrt1", [128, TT], F32)] * 2
        kqf2 = [ar.alloc(f"cdkqf{i}", [128, TT], BF16) for i in range(2)]

        pendT = None
        yield
        for tt in range(NTT):
            p.op("QSP", (lambda e, tt=tt: e.dma_start(
                out=cost.ap, in_=self.cst["cosT"][:, tt * TT:(tt + 1) * TT])),
                 writes=[cost.key], tag="cosld")
            p.op("QSP", (lambda e, tt=tt: e.dma_start(
                out=sint.ap, in_=self.cst["sinT"][:, tt * TT:(tt + 1) * TT])),
                 writes=[sint.key], tag="sinld")
            for k in range(NCH):
                p.op("QSP", (lambda e, k=k, tt=tt: e.dma_start(
                    out=catk[k].ap,
                    in_=self.dr["cat_o"][k, :, tt * TT:(tt + 1) * TT])),
                     reads=[("cat_o",)], writes=[catk[k].key], tag="cdcatld")
            for s in range(4):
                t0 = tt * TT + s * 128
                xa, x1t, y0t = xa2[s % 2], x1t2[s % 2], y0t2[s % 2]
                lnt, lnab, lnsc = lnt2[s % 2], lnab2[s % 2], lnsc2[s % 2]
                p.op("QSP", (lambda e, t0=t0, a=xa.ap: e.dma_start(
                    out=a, in_=self.xin[t0:t0 + 128, :])),
                     writes=[xa.key], tag="cdxld")
                for oc in range(2):
                    bank = oc
                    for k in range(NCH):
                        p.op("PE", (lambda e, b=self.pb[bank], k=k, s=s, oc=oc:
                                    e.matmul(b[:, 0:TT],
                                             catk[k].h[:, s * 128:(s + 1) * 128],
                                             w_outT[k].h[:, oc * 512:
                                                         (oc + 1) * 512],
                                             start=(k == 0), stop=False)),
                             reads=[catk[k].key, w_outT[k].key],
                             writes=[f"PS{bank}"], tag="ow_mm")
                    p.op("PE", (lambda e, b=self.pb[bank], oc=oc:
                                e.matmul(b[:, 0:TT], self.onesk1.h[0:1, 0:128],
                                         boutrow.h[0:1, oc * 512:(oc + 1) * 512],
                                         start=False, stop=True)),
                         reads=[self.onesk1.key, boutrow.key],
                         writes=[f"PS{bank}"], tag="ow_b")
                    p.op("DVE", (lambda e, b=self.pb[bank], oc=oc, x1t=x1t,
                                 xa=xa:
                                 e.tensor_add(x1t.h[:, oc * 512:(oc + 1) * 512],
                                              b[:, 0:TT],
                                              xa.h[:, oc * 512:(oc + 1) * 512])),
                         reads=[f"PS{bank}", xa.key], writes=[x1t.key],
                         tag="x1ev")
                p.op("QACT", (lambda e, t0=t0, a=x1t.ap: e.dma_start(
                    out=self.dr["x1"][t0:t0 + 128, :], in_=a)),
                     reads=[x1t.key], writes=[("x1",)], tag="x1st")
                self._ln_tok(x1t.ap, x1t.key, y0t.ap, y0t.key, lnsc, tag="lny0")
                p.op("QACT", (lambda e, t0=t0, a=y0t.ap: e.dma_start(
                    out=self.dr["y0"][t0:t0 + 128, :], in_=a)),
                     reads=[y0t.key], writes=[("y0",)], tag="y0st")
                self._ln_tok(y0t.ap, y0t.key, lnab.ap, lnab.key, lnsc,
                             wmul=(ln1bc, lnt), tag="lna")
                if pendT is not None:
                    pendT()
                pendT = (lambda lnab=lnab, s=s:
                         self._transpose_tok_to_f(lnab, lnab.key, lnaf,
                                                  s * 128, tag="lnaT"))
                yield
            pendT()
            pendT = None
            lnaf_reads = [(lnaf[k].key, "q", s * 128)
                          for k in range(NCH) for s in range(4)]
            for m2 in range(2):
                for wt, dst in ((kv_dT, kvlat), (q_dT, qlat)):
                    bank = 2 + m2
                    for k in range(NCH):
                        p.op("PE", (lambda e, b=self.pb[bank], k=k, m2=m2, wt=wt:
                                    e.matmul(b[:, 0:TT],
                                             wt[k].h[:, m2 * 128:(m2 + 1) * 128],
                                             lnaf[k].ap, start=(k == 0),
                                             stop=(k == NCH - 1))),
                             reads=[wt[k].key] + lnaf_reads,
                             writes=[f"PS{bank}"], tag="lat_mm")
                    p.op("ACT", (lambda e, b=self.pb[bank], d=dst[m2]:
                                 e.copy(d.ap, b[:, 0:TT])),
                         reads=[f"PS{bank}"], writes=[dst[m2].key], tag="lat_ev")
            for s in range(4):
                vb = vts[s % 2]
                for nb in range(2):
                    bank = 2 + nb
                    for k2 in range(2):
                        p.op("PE", (lambda e, b=self.pb[bank], k2=k2, s=s, nb=nb:
                                    e.matmul(b[:, 0:TT],
                                             kvlat[k2].h[:, s * 128:(s + 1) * 128],
                                             v_uT[k2].h[:, nb * 512:(nb + 1) * 512],
                                             start=(k2 == 0), stop=(k2 == 1))),
                             reads=[kvlat[k2].key, v_uT[k2].key],
                             writes=[f"PS{bank}"], tag="v_mm")
                    p.op("DVE", (lambda e, b=self.pb[bank], vb=vb, nb=nb:
                                 e.tensor_copy(
                                     vb.h[:, nb * 8 * 65:(nb + 1) * 8 * 65]
                                     .rearrange("p (h x) -> p h x", x=65)[:, :, 0:64],
                                     b[:, 0:TT])),
                         reads=[f"PS{bank}"], writes=[vb.key], tag="v_ev")
                p.op("QACT", (lambda e, vb=vb, jt=tt * 4 + s: e.dma_start(
                    out=self.dr["vti"][jt], in_=vb.ap)),
                     reads=[vb.key, (vb.key, "ones")], writes=[("vti",)],
                     tag="vti_st")
            yield
            for c in range(NCH):
                for wi, (which, dst) in enumerate((("k", self.dr["kfull"]),
                                                   ("q", self.dr["qfull"]))):
                    par = (2 * c + wi) % 2
                    pre_sb = pre2[par]
                    rt1 = rt12[par]
                    kqf = kqf2[par]
                    bank = 4 if par == 0 else 2
                    if which == "k":
                        for k2 in range(2):
                            p.op("PE", (lambda e, b=self.pb[bank], c=c, k2=k2:
                                        e.matmul(b[:, 0:TT], KUSEL[c][k2].ap,
                                                 kvlat[k2].h[:, :],
                                                 start=(k2 == 0), stop=False)),
                                 reads=[KUSEL[c][k2].key, kvlat[k2].key],
                                 writes=[f"PS{bank}"], tag="kpre_mm")
                        for k in range(NCH):
                            p.op("PE", (lambda e, b=self.pb[bank], c=c, k=k:
                                        e.matmul(b[:, 0:TT], RKSEL[c][k].ap,
                                                 lnaf[k].ap, start=False,
                                                 stop=(k == NCH - 1))),
                                 reads=[RKSEL[c][k].key] + lnaf_reads,
                                 writes=[f"PS{bank}"], tag="kpre_mm2")
                    else:
                        for k2 in range(2):
                            p.op("PE", (lambda e, b=self.pb[bank], c=c, k2=k2:
                                        e.matmul(b[:, 0:TT],
                                                 QCOMB[k2].h[:, c * 128:
                                                             (c + 1) * 128],
                                                 qlat[k2].h[:, :],
                                                 start=(k2 == 0), stop=(k2 == 1))),
                                 reads=[QCOMB[k2].key, qlat[k2].key],
                                 writes=[f"PS{bank}"], tag="qpre_mm")
                    rbank = 5 if par == 0 else 3
                    p.op("ACT", (lambda e, b=self.pb[bank], a=pre_sb.ap:
                                 e.copy(a, b[:, 0:TT])),
                         reads=[f"PS{bank}"], writes=[pre_sb.key], tag="pre_ev")
                    p.op("PE", (lambda e, b=self.pb[rbank], a=pre_sb.ap:
                                e.matmul(b[:, 0:TT], self.rmat.ap, a,
                                         start=True, stop=True)),
                         reads=[self.rmat.key, pre_sb.key], writes=[f"PS{rbank}"],
                         tag="rot_mm")
                    p.op("DVE", (lambda e, a=rt1.ap, b2=pre_sb.ap:
                                 e.tensor_mul(a, b2, cost.ap)),
                         reads=[pre_sb.key, cost.key], writes=[rt1.key],
                         tag="rope1")
                    p.op("DVE", (lambda e, b=self.pb[rbank]:
                                 e.tensor_mul(b[:, 0:TT], b[:, 0:TT], sint.ap)),
                         reads=[sint.key, f"PS{rbank}"], writes=[f"PS{rbank}"],
                         tag="rope2")
                    p.op("DVE", (lambda e, b=self.pb[rbank], a=kqf.ap, r=rt1.ap:
                                 e.tensor_add(a, r, b[:, 0:TT])),
                         reads=[rt1.key, f"PS{rbank}"], writes=[kqf.key],
                         tag="rope3")
                    p.op("QACT", (lambda e, dst=dst, c=c, tt=tt, a=kqf.ap:
                                  e.dma_start(
                        out=dst[c, :, tt * TT:(tt + 1) * TT], in_=a)),
                         reads=[kqf.key],
                         writes=[("kfull",) if which == "k" else ("qfull",)],
                         tag="kqf_st")
                if wi == 1 and c % 2 == 1:
                    yield
        self.dbg_dump_dram("x1", self.dr["x1"], [("x1",)])
        self.dbg_dump_dram("y0", self.dr["y0"], [("y0",)])
        self.dbg_dump_dram("kfull", self.dr["kfull"], [("kfull",)])
        self.dbg_dump_dram("qfull", self.dr["qfull"], [("qfull",)])
        self.dbg_dump_dram("vti", self.dr["vti"], [("vti",)])

    def phase_E_iter(self):
        nc, p, ar = self.nc, self.p, self.ar
        kch = [ar.alloc(f"ekc{i}", [128, T], BF16) for i in range(2)]
        vti = [ar.alloc(f"evt{j}", [128, NH * 65], BF16) for j in range(T // 128)]
        qch = [ar.alloc(f"eq{i}", [128, TT], BF16) for i in range(2)]
        ptile = [ar.alloc(f"ep{i}", [128, TT], BF16) for i in range(6)]
        catb = [ar.alloc(f"ecat{i}", [64, TT], BF16) for i in range(4)]
        lrow = [ar.alloc(f"elr{i}", [1, TT], F32) for i in range(4)]
        pend_tail = []
        yield
        for qg in range(NTT):
            for j in range(4):
                jt = qg * 4 + j
                p.op("QSP", (lambda e, jt=jt: e.dma_start(
                    out=vti[jt].ap, in_=self.dr["vti"][jt])),
                     reads=[("vti",)], writes=[vti[jt].key], tag="evtld")
            for c in range(NCH):
                qb = qch[c % 2]
                kb = kch[c % 2]
                p.op("QSP", (lambda e, kb=kb, c=c, qg=qg: e.dma_start(
                    out=kb.h[:, 0:(qg + 1) * TT],
                    in_=self.dr["kfull"][c, :, 0:(qg + 1) * TT])),
                     reads=[("kfull",)], writes=[kb.key], tag="ekld")
                p.op("QSP", (lambda e, qb=qb, c=c, qg=qg: e.dma_start(
                    out=qb.ap, in_=self.dr["qfull"][c, :, qg * TT:(qg + 1) * TT])),
                     reads=[("qfull",)], writes=[qb.key], tag="eqld")
                njt = 4 * qg + 4

                def _qk(jt, c=c, qb=qb, kb=kb, qg=qg):
                    s0 = jt - 4 * qg
                    for half in range(2):
                        sb = 2 * (jt % 2) + half
                        hh = half * 64
                        pt = ptile[2 * (jt % 3) + half]
                        p.op("PE", (lambda e, b=self.pb[sb], kb=kb, hh=hh,
                                    jt=jt, qb=qb:
                                    e.matmul(b[:, 0:TT],
                                             kb.h[hh:hh + 64,
                                                  jt * 128:(jt + 1) * 128],
                                             qb.h[hh:hh + 64, :],
                                             start=True, stop=True)),
                             reads=[kb.key, qb.key], writes=[f"PS{sb}"],
                             tag="iqk")
                        p.op("ACT", (lambda e, b=self.pb[sb], pt=pt:
                                     e.activation(pt.ap, b[:, 0:TT], AF.Exp,
                                                  scale=self.sc18.ap)),
                             reads=[f"PS{sb}", self.sc18.key], writes=[pt.key],
                             tag="iexp")
                        if jt >= 4 * qg:
                            if s0 > 0:
                                p.op("DVE", (lambda e, pt=pt, s0=s0: e.memset(
                                    pt.h[:, 0:s0 * 128], 0.0)),
                                     writes=[pt.key], tag="izero")
                            p.op("DVE", (lambda e, pt=pt, s0=s0:
                                         e.tensor_mul(
                                             pt.h[:, s0 * 128:(s0 + 1) * 128],
                                             pt.h[:, s0 * 128:(s0 + 1) * 128],
                                             self.mdiag.ap)),
                                 reads=[self.mdiag.key], writes=[pt.key],
                                 tag="imask")

                def _av(jt, c=c, njt=njt):
                    for half in range(2):
                        ab = 4 + half
                        h = 2 * c + half
                        pt = ptile[2 * (jt % 3) + half]
                        p.op("PE", (lambda e, b=self.pb[ab], jt=jt, h=h, pt=pt,
                                    first=(jt == 0), last=(jt == njt - 1):
                                    e.matmul(b[0:65, 0:TT],
                                             vti[jt].h[:, h * 65:h * 65 + 65],
                                             pt.ap, start=first, stop=last)),
                             reads=[vti[jt].key, pt.key], writes=[f"PS{ab}"],
                             tag="iav")

                for jt in range(njt + 2):
                    if jt < njt:
                        _qk(jt)
                    if jt == 1 and pend_tail:
                        for f in pend_tail:
                            f()
                        pend_tail = []
                    if jt >= 2:
                        _av(jt - 2)
                for half in range(2):
                    ab = 4 + half
                    spb = 2 * (njt % 2) + half
                    hh = half * 64
                    cb = catb[2 * (c % 2) + half]
                    lr = lrow[2 * (c % 2) + half]
                    p.op("ACT", (lambda e, b=self.pb[ab], cb=cb:
                                 e.copy(cb.ap, b[0:64, 0:TT])),
                         reads=[f"PS{ab}"], writes=[(cb.key, "raw")], tag="icat")
                    p.op("DVE", (lambda e, b=self.pb[ab], lr=lr:
                                 e.tensor_copy(lr.ap, b[64:65, 0:TT])),
                         reads=[f"PS{ab}"], writes=[(lr.key, "l")], tag="ilcp")
                    p.op("DVE", (lambda e, lr=lr: e.reciprocal(lr.ap, lr.ap)),
                         reads=[(lr.key, "l")], writes=[lr.key], tag="ilinv")

                    def _tail(spb=spb, cb=cb, lr=lr, c=c, hh=hh, qg=qg):
                        p.op("PE", (lambda e, b=self.pb[spb], lr=lr:
                                    e.matmul(b[0:64, 0:TT],
                                             self.onesk1.h[0:1, 0:64],
                                             lr.ap, start=True, stop=True)),
                             reads=[self.onesk1.key, lr.key],
                             writes=[f"PS{spb}"], tag="ibc")
                        p.op("DVE", (lambda e, b=self.pb[spb], cb=cb:
                                     e.tensor_mul(cb.ap, cb.ap, b[0:64, 0:TT])),
                             reads=[f"PS{spb}", (cb.key, "raw")],
                             writes=[cb.key], tag="inorm")
                        p.op("QACT", (lambda e, cb=cb, c=c, hh=hh, qg=qg:
                                      e.dma_start(
                                          out=self.dr["cat_i"][c, hh:hh + 64,
                                                               qg * TT:
                                                               (qg + 1) * TT],
                                          in_=cb.ap)),
                             reads=[cb.key], writes=[("cat_i",)], tag="icatst")
                    pend_tail.append(_tail)
                if c == NCH - 1:
                    for f in pend_tail:
                        f()
                    pend_tail = []
                yield
        self.dbg_dump_dram("cat_i", self.dr["cat_i"], [("cat_i",)])

    def phase_F_iter(self):
        nc, p, ar = self.nc, self.p, self.ar
        o_wT = [ar.alloc(f"owT{k}", [128, D], BF16) for k in range(NCH)]
        for k in range(NCH):
            p.op("QACT", (lambda e, a=o_wT[k].ap,
                          b=self.dr["ow_bf"][:, k * 128:(k + 1) * 128]:
                          e.dma_start_transpose(a, b)),
                 reads=[("cast", "ow_bf")], writes=[o_wT[k].key], tag="owT")
        rnat = ar.alloc("rnat", [7, D], BF16)
        p.op("QACT", (lambda e: e.dma_start(out=rnat.ap,
                                            in_=self.dr["router_bf"])),
             reads=[("cast", "router_bf")], writes=[rnat.key])
        rwT = [ar.alloc(f"rwT{k}", [128, NE], BF16) for k in range(NCH)]
        for k in range(NCH):
            bank = 6 + k % 2
            p.op("PE", (lambda e, b=self.pb[bank], k=k:
                        e.transpose(b[:, 0:NE],
                                    rnat.h[0:NE, k * 128:(k + 1) * 128],
                                    self.ident.h[0:NE, 0:NE])),
                 reads=[rnat.key, self.ident.key], writes=[f"PS{bank}"])
            p.op("DVE", (lambda e, b=self.pb[bank], k=k:
                         e.tensor_copy(rwT[k].ap, b[:, 0:NE])),
                 reads=[f"PS{bank}"], writes=[rwT[k].key])
        biasrow = ar.alloc("biasrow", [1, NE], F32)
        p.op("QSP", (lambda e: e.dma_start(
            out=biasrow.ap,
            in_=self.win["routing_bias"].rearrange("(a d) -> a d", a=1))),
             writes=[biasrow.key])
        ln2row = ar.alloc("ln2row", [1, D], F32)
        p.op("QSP", (lambda e: e.dma_start(
            out=ln2row.ap, in_=self.win["ln2_w"].rearrange("(a d) -> a d", a=1))),
             writes=[ln2row.key])
        ln2bc = ar.alloc("ln2bc", [128, D], F32)
        for half in range(2):
            p.op("PE", (lambda e, b=self.pb[5], half=half:
                        e.matmul(b[:, 0:TT], self.onesk1.h[0:1, 0:128],
                                 ln2row.h[0:1, half * 512:(half + 1) * 512],
                                 start=True, stop=True)),
                 reads=[self.onesk1.key, ln2row.key], writes=["PS5"])
            p.op("DVE", (lambda e, b=self.pb[5], half=half:
                         e.tensor_copy(ln2bc.h[:, half * 512:(half + 1) * 512],
                                       b[:, 0:TT])),
                 reads=["PS5"], writes=[ln2bc.key])
        catk = [ar.alloc(f"fcat{k}", [128, TT], BF16) for k in range(NCH)]
        y0t = ar.alloc("fy0", [128, D], F32)
        x1a = ar.alloc("fx1", [128, D], F32)
        y1t = ar.alloc("fy1", [128, D], F32)
        baset = ar.alloc("fbase", [128, D], F32)
        lnt = ar.alloc("flnt", [128, D], F32)
        xtt2 = [ar.alloc(f"fxtt{i}", [128, D], BF16) for i in range(2)]
        lnsc = self._mk_ln_scratch(ar, "f")
        xtf = [ar.alloc(f"fxtf{k}", [128, TT], BF16) for k in range(NCH)]
        probs = ar.alloc("fprobs", [128, NE], F32)
        eqm = ar.alloc("feqm", [128, NE], F32)
        wraw = ar.alloc("fwraw", [128, NE], F32)
        m1 = ar.alloc("fm1", [128, 1], F32)
        m2 = ar.alloc("fm2", [128, 1], F32)
        den = ar.alloc("fden", [128, 1], F32)
        w7 = ar.alloc("fw7", [7, 128], F32)

        def _xt_router(xtt, s, tidx):
                self._transpose_tok_to_f(xtt, xtt.key, xtf, s * 128, tag="xtT")
                xtf_reads = [(xtf[k].key, "q", s * 128) for k in range(NCH)]
                bank = 4
                for k in range(NCH):
                    p.op("PE", (lambda e, b=self.pb[bank], k=k, s=s:
                                e.matmul(b[:, 0:NE],
                                         xtf[k].h[:, s * 128:(s + 1) * 128],
                                         rwT[k].ap, start=(k == 0), stop=False)),
                         reads=[rwT[k].key] + xtf_reads, writes=[f"PS{bank}"],
                         tag="rt_mm")
                p.op("PE", (lambda e, b=self.pb[bank]:
                            e.matmul(b[:, 0:NE], self.onesk1.h[0:1, 0:128],
                                     biasrow.ap, start=False, stop=True)),
                     reads=[self.onesk1.key, biasrow.key], writes=[f"PS{bank}"],
                     tag="rt_b")
                p.op("ACT", (lambda e, b=self.pb[bank]:
                             e.activation(probs.ap, b[:, 0:NE], AF.Sigmoid)),
                     reads=[f"PS{bank}"], writes=[probs.key], tag="rt_sig")
                p.op("DVE", (lambda e: e.reduce_max(m1.ap, probs.ap, axis=AX.X)),
                     reads=[probs.key], writes=[m1.key])
                p.op("DVE", (lambda e: e.tensor_scalar(
                    eqm.ap, probs.ap, m1.ap, None, ALU.is_equal)),
                     reads=[probs.key, m1.key], writes=[(eqm.key, "e")])
                p.op("DVE", (lambda e: e.tensor_scalar(
                    eqm.ap, eqm.ap, -1e30, None, ALU.mult)),
                     reads=[(eqm.key, "e")], writes=[eqm.key])
                p.op("DVE", (lambda e: e.tensor_add(eqm.ap, eqm.ap, probs.ap)),
                     reads=[probs.key, eqm.key], writes=[(eqm.key, "p2")])
                p.op("DVE", (lambda e: e.reduce_max(m2.ap, eqm.ap, axis=AX.X)),
                     reads=[(eqm.key, "p2")], writes=[m2.key])
                p.op("DVE", (lambda e: e.tensor_scalar(
                    wraw.ap, probs.ap, m2.ap, None, ALU.is_ge)),
                     reads=[probs.key, m2.key], writes=[wraw.key])
                p.op("DVE", (lambda e: e.tensor_mul(wraw.ap, wraw.ap, probs.ap)),
                     reads=[probs.key], writes=[wraw.key])
                p.op("DVE", (lambda e: e.reduce_sum(den.ap, wraw.ap, axis=AX.X)),
                     reads=[wraw.key], writes=[(den.key, "d")])
                p.op("DVE", (lambda e: e.reciprocal(den.ap, den.ap)),
                     reads=[(den.key, "d")], writes=[den.key])
                p.op("DVE", (lambda e: e.tensor_scalar(
                    wraw.ap, wraw.ap, den.ap, None, ALU.mult)),
                     reads=[den.key, wraw.key], writes=[wraw.key], tag="rt_w")
                p.op("QACT", (lambda e, tidx=tidx: e.dma_start(
                    out=self.dr["wgt_tm"][tidx], in_=wraw.ap)),
                     reads=[wraw.key], writes=[("wgt_tm",)], tag="wgttmst")
                bank5 = 5
                p.op("PE", (lambda e, b=self.pb[bank5]:
                            e.transpose(b[0:NE, 0:128], wraw.ap,
                                        self.ident32.ap)),
                     reads=[wraw.key, self.ident32.key], writes=[f"PS{bank5}"],
                     tag="rt_T")
                p.op("DVE", (lambda e, b=self.pb[bank5]:
                             e.tensor_copy(w7.ap, b[0:NE, 0:128])),
                     reads=[f"PS{bank5}"], writes=[w7.key])
                p.op("QACT", (lambda e, tidx=tidx: e.dma_start(
                    out=self.dr["wgt"][:, tidx * 128:(tidx + 1) * 128],
                    in_=w7.ap)),
                     reads=[w7.key], writes=[("wgt",)], tag="wgtst")

        pendR = None
        yield
        for tt in range(NTT):
            for k in range(NCH):
                p.op("QSP", (lambda e, k=k, tt=tt: e.dma_start(
                    out=catk[k].ap,
                    in_=self.dr["cat_i"][k, :, tt * TT:(tt + 1) * TT])),
                     reads=[("cat_i",)], writes=[catk[k].key], tag="fcatld")
            for s in range(4):
                t0 = tt * TT + s * 128
                tidx = tt * 4 + s
                p.op("QSP", (lambda e, t0=t0: e.dma_start(
                    out=y0t.ap, in_=self.dr["y0"][t0:t0 + 128, :])),
                     reads=[("y0",)], writes=[y0t.key], tag="fy0ld")
                p.op("QSP", (lambda e, t0=t0: e.dma_start(
                    out=x1a.ap, in_=self.dr["x1"][t0:t0 + 128, :])),
                     reads=[("x1",)], writes=[x1a.key], tag="fx1ld")
                for oc in range(2):
                    bank = oc
                    for k in range(NCH):
                        p.op("PE", (lambda e, b=self.pb[bank], k=k, s=s, oc=oc:
                                    e.matmul(b[:, 0:TT],
                                             catk[k].h[:, s * 128:(s + 1) * 128],
                                             o_wT[k].h[:, oc * 512:(oc + 1) * 512],
                                             start=(k == 0), stop=(k == NCH - 1))),
                             reads=[catk[k].key, o_wT[k].key],
                             writes=[f"PS{bank}"], tag="iow_mm")
                    p.op("DVE", (lambda e, b=self.pb[bank], oc=oc:
                                 e.tensor_add(y1t.h[:, oc * 512:(oc + 1) * 512],
                                              b[:, 0:TT],
                                              y0t.h[:, oc * 512:(oc + 1) * 512])),
                         reads=[f"PS{bank}", y0t.key], writes=[y1t.key],
                         tag="y1ev")
                p.op("DVE", (lambda e: e.tensor_add(baset.ap, y1t.ap, x1a.ap)),
                     reads=[y1t.key, x1a.key], writes=[baset.key], tag="basev")
                p.op("QACT", (lambda e, t0=t0: e.dma_start(
                    out=self.out[t0:t0 + 128, :], in_=baset.ap)),
                     reads=[baset.key], writes=[("acc", tidx, 0), ("acc", tidx, 1)],
                     tag="basest")
                xtt = xtt2[s % 2]
                self._ln_tok(y1t.ap, y1t.key, xtt.ap, xtt.key, lnsc,
                             wmul=(ln2bc, lnt), tag="lnxt")
                if pendR is not None:
                    pendR()
                pendR = (lambda xtt=xtt, s=s, tidx=tidx: _xt_router(xtt, s, tidx))
                yield
            pendR()
            pendR = None

            for k in range(NCH):
                p.op("QACT", (lambda e, k=k, tt=tt: e.dma_start(
                    out=self.dr["xtf"][k, :, tt * TT:(tt + 1) * TT],
                    in_=xtf[k].ap)),
                     reads=[(xtf[k].key, "q", s2 * 128) for s2 in range(4)],
                     writes=[("xtf",)], tag="xtfst")
            yield
        self.dbg_dump_dram("xtf", self.dr["xtf"], [("xtf",)])
        self.dbg_dump_dram("wgt", self.dr["wgt"], [("wgt",)])
        if "base" in self.dbg:
            t = self.dbg_out("base", [T, D], F32)
            p.op("QSP", (lambda e, t=t: e.dma_start(out=t, in_=self.out)),
                 reads=[("acc", tidx2, oc) for tidx2 in range(32)
                        for oc in range(2)], tag="dbgbase")

    def phase_G(self):
        nc, p, ar = self.nc, self.p, self.ar
        ar.mark()
        gT2 = [[ar.alloc(f"ggT{pp}_{k}", [128, IE], BF16) for k in range(NCH)]
               for pp in range(2)]
        uT2 = [[ar.alloc(f"guT{pp}_{k}", [128, IE], BF16) for k in range(NCH)]
               for pp in range(2)]
        dT = [ar.alloc(f"gdT{k}", [128, D], BF16) for k in range(IE // 128)]
        xtfk2 = [[ar.alloc(f"gxt{pp}_{k}", [128, TT], BF16) for k in range(NCH)]
                 for pp in range(2)]
        h_sb = [ar.alloc(f"gh{m}", [128, TT], BF16) for m in range(IE // 128)]
        gs = ar.alloc("ggs", [128, TT], BF16)
        wtm = ar.alloc("gwtm", [128, 32 * NE], F32)
        accb2 = [ar.alloc(f"gacc{i}", [128, TT], F32) for i in range(2)]
        p.op("QPOOL", (lambda e: e.dma_start(
            out=wtm.h[:, :].rearrange("p (t e) -> p t e", e=NE),
            in_=self.dr["wgt_tm"].rearrange("t p e -> p t e"),
            allow_slow_non_contiguous=True)),
             reads=[("wgt_tm",)], writes=[wtm.key], tag="gwtmld")
        for eidx, e_i in enumerate(self.moe_experts):
            gT, uT = gT2[eidx % 2], uT2[eidx % 2]
            routed = e_i > 0
            if not routed:
                gsrc, usrc, dsrc = (self.dr["sg_bf"], self.dr["su_bf"],
                                    self.dr["sd_bf"])
                gck, uck, dck = (("cast", "sg_bf"), ("cast", "su_bf"),
                                 ("cast", "sd_bf"))
            else:
                ee = e_i - 1
                gsrc, usrc, dsrc = (self.dr["rg_bf"][ee], self.dr["ru_bf"][ee],
                                    self.dr["rd_bf"][ee])
                gck, uck, dck = (("cast", "rg_bf", ee), ("cast", "ru_bf", ee),
                                 ("cast", "rd_bf", ee))
            for k in range(NCH):
                p.op("QSP", (lambda e, a=gT[k].ap,
                             b=gsrc[:, k * 128:(k + 1) * 128], ck=gck:
                             e.dma_start_transpose(a, b)),
                     reads=[gck], writes=[gT[k].key], tag="gTld")
                p.op("QSP", (lambda e, a=uT[k].ap,
                             b=usrc[:, k * 128:(k + 1) * 128], ck=uck:
                             e.dma_start_transpose(a, b)),
                     reads=[uck], writes=[uT[k].key], tag="uTld")
            for k in range(IE // 128):
                p.op("QSP", (lambda e, a=dT[k].ap,
                             b=dsrc[:, k * 128:(k + 1) * 128], ck=dck:
                             e.dma_start_transpose(a, b)),
                     reads=[dck], writes=[dT[k].key], tag="dTld")
            for tt in range(NTT):
                xtfk = xtfk2[tt % 2]
                for k in range(NCH):
                    p.op("QPOOL", (lambda e, a=xtfk[k].ap, k=k, tt=tt: e.dma_start(
                        out=a,
                        in_=self.dr["xtf"][k, :, tt * TT:(tt + 1) * TT])),
                         reads=[("xtf",)], writes=[xtfk[k].key], tag="gxld")
                for m in range(IE // 128):
                    gb, ub = m % 2 * 2, m % 2 * 2 + 1
                    for k in range(NCH):
                        p.op("PE", (lambda e, b=self.pb[gb], k=k, m=m,
                                    w=gT[k], xt=xtfk[k]:
                                    e.matmul(b[:, 0:TT],
                                             w.h[:, m * 128:(m + 1) * 128],
                                             xt.ap, start=(k == 0),
                                             stop=(k == NCH - 1))),
                             reads=[gT[k].key, xtfk[k].key], writes=[f"PS{gb}"],
                             tag="g_mm")
                    for k in range(NCH):
                        p.op("PE", (lambda e, b=self.pb[ub], k=k, m=m,
                                    w=uT[k], xt=xtfk[k]:
                                    e.matmul(b[:, 0:TT],
                                             w.h[:, m * 128:(m + 1) * 128],
                                             xt.ap, start=(k == 0),
                                             stop=(k == NCH - 1))),
                             reads=[uT[k].key, xtfk[k].key], writes=[f"PS{ub}"],
                             tag="u_mm")
                    p.op("ACT", (lambda e, b=self.pb[gb]:
                                 e.activation(gs.ap, b[:, 0:TT], AF.Silu)),
                         reads=[f"PS{gb}"], writes=[gs.key], tag="silu")
                    p.op("DVE", (lambda e, b=self.pb[ub], m=m:
                                 e.tensor_mul(h_sb[m].ap, gs.ap, b[:, 0:TT])),
                         reads=[gs.key, f"PS{ub}"], writes=[(h_sb[m].key, "h")],
                         tag="hmul")

                for s in range(4):
                    tidx = tt * 4 + s
                    for oc in range(2):
                        bank = 4 + oc
                        for k in range(IE // 128):
                            p.op("PE", (lambda e, b=self.pb[bank], k=k, s=s,
                                        oc=oc:
                                        e.matmul(b[:, 0:TT],
                                                 h_sb[k].h[:, s * 128:
                                                           (s + 1) * 128],
                                                 dT[k].h[:, oc * 512:
                                                         (oc + 1) * 512],
                                                 start=(k == 0),
                                                 stop=(k == IE // 128 - 1))),
                                 reads=[(h_sb[k].key, "h"), dT[k].key],
                                 writes=[f"PS{bank}"], tag="d_mm")
                        accb = accb2[(s * 2 + oc) % 2]
                        p.op("QPOOL", (lambda e, a=accb.ap, tidx=tidx, oc=oc:
                                       e.dma_start(
                            out=a,
                            in_=self.out[tidx * 128:(tidx + 1) * 128,
                                         oc * 512:(oc + 1) * 512])),
                             reads=[("acc", tidx, oc)], writes=[accb.key],
                             tag="accld")
                        if routed:
                            p.op("DVE", (lambda e, b=self.pb[bank], a=accb.ap,
                                         wc=wtm.h[:, tidx * NE + e_i - 1:
                                                  tidx * NE + e_i]:
                                         e.scalar_tensor_tensor(
                                             a, b[:, 0:TT], wc, a,
                                             ALU.mult, ALU.add)),
                                 reads=[f"PS{bank}", wtm.key],
                                 writes=[accb.key], tag="accadd")
                        else:
                            p.op("DVE", (lambda e, b=self.pb[bank], a=accb.ap:
                                         e.tensor_add(a, b[:, 0:TT], a)),
                                 reads=[f"PS{bank}"], writes=[accb.key],
                                 tag="accadd")
                        p.op("QACT", (lambda e, a=accb.ap, tidx=tidx, oc=oc:
                                      e.dma_start(
                            out=self.out[tidx * 128:(tidx + 1) * 128,
                                         oc * 512:(oc + 1) * 512],
                            in_=a)),
                             reads=[accb.key], writes=[("acc", tidx, oc)],
                             tag="accst")
        ar.release()


def _in_map(core, inputs):
    g = lambda n: np.ascontiguousarray(inputs[n])
    m = {
        "xin": np.ascontiguousarray(inputs["L0_attn_x"][core]),
        "w_qkv": g("L0_attn_w_qkv"),
        "w_conv": np.ascontiguousarray(
            inputs["L0_attn_w_conv"].reshape(D, DH * CF)),
        "b_conv": g("L0_attn_b_conv"),
        "null_k": np.ascontiguousarray(inputs["L0_attn_null_k"].reshape(D)),
        "null_v": np.ascontiguousarray(inputs["L0_attn_null_v"].reshape(D)),
        "w_out": g("L0_attn_w_out"), "b_out": g("L0_attn_b_out"),
        "ln1_w": g("L0_ffn_ln1_w"), "ln2_w": g("L0_ffn_ln2_w"),
        "kv_d": g("L0_ffn_kv_d"), "q_d": g("L0_ffn_q_d"),
        "k_u": g("L0_ffn_k_u"), "q_u": g("L0_ffn_q_u"), "v_u": g("L0_ffn_v_u"),
        "rope_q": g("L0_ffn_rope_q_w"), "rope_k": g("L0_ffn_rope_k_w"),
        "o_w": g("L0_ffn_o_w"), "router_w": g("L0_ffn_router_w"),
        "routing_bias": g("L0_ffn_routing_bias"),
        "sh_gate": np.ascontiguousarray(inputs["L0_ffn_shared_gate"][0]),
        "sh_up": np.ascontiguousarray(inputs["L0_ffn_shared_up"][0]),
        "sh_down": np.ascontiguousarray(inputs["L0_ffn_shared_down"][0]),
        "r_gate": g("L0_ffn_routed_gate"), "r_up": g("L0_ffn_routed_up"),
        "r_down": g("L0_ffn_routed_down"),
    }
    return m


def kernel(**inputs):
    b = Build()
    nc = b.build()
    in_maps = [_in_map(c, inputs) for c in range(8)]
    res = run_bass_kernel_spmd(nc, in_maps, core_ids=list(range(8)))
    return np.stack([res.results[c]["out"] for c in range(8)], axis=0)


# version 16 (final)
# speedup vs baseline: 1.0323x; 1.0323x over previous
import contextlib
import numpy as np
import ml_dtypes
import concourse.bass as bass
import concourse.mybir as mybir
from concourse.bass_utils import run_bass_kernel_spmd

F32 = mybir.dt.float32
BF16 = mybir.dt.bfloat16
AF = mybir.ActivationFunctionType
ALU = mybir.AluOpType
AX = mybir.AxisListType

T = 4096
D = 1024
NCH = 8
TT = 512
NTT = T // TT
NH = 16
DH = 64
CF = 3
NCONV = (T + 2) // CF
JTOT = NCONV + 1
NJT = (JTOT + 127) // 128
RD = 32
LAT = 256
NE = 7
IE = 2048
EPS = 1e-5

DT_SIZE = {F32: 4, BF16: 2}
ALL_PHASES = ("A", "A2", "B", "CD", "E", "F", "G")


def _host_consts():
    c = {}
    c["ident_bf"] = np.eye(128, dtype=ml_dtypes.bfloat16)
    c["ident_f32"] = np.eye(128, dtype=np.float32)
    inv_freq = 1.0 / 10000.0 ** (np.arange(0, RD, 2, dtype=np.float64) / RD)
    fr = np.arange(T, dtype=np.float64)[:, None] * inv_freq[None, :]
    cos16 = np.cos(fr).astype(np.float32).T
    sin16 = np.sin(fr).astype(np.float32).T
    cosT = np.ones((128, T), dtype=np.float32)
    sinT = np.zeros((128, T), dtype=np.float32)
    for half in (0, 64):
        cosT[half:half + 16] = cos16
        cosT[half + 16:half + 32] = cos16
        sinT[half:half + 16] = -sin16
        sinT[half + 16:half + 32] = sin16
    c["cosT"], c["sinT"] = cosT, sinT
    R = np.zeros((128, 128), dtype=np.float32)
    for half in (0, 64):
        for i in range(16):
            R[half + 16 + i, half + i] = 1.0
            R[half + i, half + 16 + i] = 1.0
    c["Rmat_bf"] = R.astype(ml_dtypes.bfloat16)
    jl = np.arange(128)[:, None]
    ql = np.arange(128)[None, :]
    c["mask_diag_bf"] = (ql >= jl).astype(ml_dtypes.bfloat16)
    part = {}
    kind = np.zeros((NJT, NTT), dtype=np.int8)
    for jt in range(NJT):
        jj = np.arange(jt * 128, jt * 128 + 128)
        for qg in range(NTT):
            qq = np.arange(qg * TT, qg * TT + TT)
            allow = (qq[None, :] >= 3 * jj[:, None]) | (jj[:, None] >= NCONV)
            valid = allow[jj < JTOT]
            if valid.all():
                kind[jt, qg] = 1
            elif not valid.any():
                kind[jt, qg] = 0
            else:
                kind[jt, qg] = 2
                part[(jt, qg)] = allow.astype(ml_dtypes.bfloat16)
    c["outer_kind"], c["outer_masks"] = kind, part
    return c


class _Op:
    __slots__ = ("dom", "idx", "fn", "waits", "marked", "cvec", "tag")


class Prog:
    ENGS = ("PE", "DVE", "ACT", "POOL", "SP")
    QUEUES = ("QSP", "QACT", "QPOOL")
    QISSUE = {"QSP": "SP", "QACT": "ACT", "QPOOL": "POOL"}
    NCHAN = {"QSP": 12, "QACT": 8, "QPOOL": 4}

    def __init__(self):
        self.qdoms = tuple(f"{q}#{i}" for q in self.QUEUES
                           for i in range(self.NCHAN[q]))
        self.streams = {e: [] for e in self.ENGS}
        self.counts = {d: 0 for d in self.ENGS + self.qdoms}
        self.eng_clock = {e: {} for e in self.ENGS}
        self.res = {}
        self.ops = {d: [] for d in self.ENGS + self.qdoms}
        self.sems = {}
        self.rr = {q: 0 for q in self.QUEUES}
        self.last_dma = {}

    def op(self, where, fn, reads=(), writes=(), tag=""):
        if where in self.QUEUES:
            ch = self.rr[where]
            self.rr[where] = (ch + 1) % self.NCHAN[where]
            dom, eng = f"{where}#{ch}", self.QISSUE[where]
        else:
            dom = eng = where
        deps = []
        if dom in self.counts and dom not in self.ENGS:
            prev = self.last_dma.get(dom)
            if prev is not None:
                deps.append(prev)
        def _base_init(k):
            base = k[0] if (isinstance(k, tuple) and isinstance(k[0], tuple)) \
                else None
            if base is not None:
                eb = self.res.get(base)
                if eb:
                    deps.extend(eb.get("init", ()))

        for r in reads:
            e = self.res.get(r)
            if e:
                if e["w"] is not None:
                    deps.append(e["w"])
                deps.extend(e.get("init", ()))
            _base_init(r)
        for w in writes:
            e = self.res.get(w)
            if e:
                if e["w"] is not None:
                    deps.append(e["w"])
                deps.extend(e["r"])
                deps.extend(e.get("init", ()))
            _base_init(w)
        waits = []
        ec = self.eng_clock[eng]
        for (ddom, didx, dcvec) in deps:
            if ddom == "PE" and dom == "PE":
                continue
            if ec.get(ddom, 0) >= didx + 1:
                continue
            waits.append((ddom, didx))
            for d2, v in dcvec.items():
                if v > ec.get(d2, 0):
                    ec[d2] = v
            if ec.get(ddom, 0) < didx + 1:
                ec[ddom] = didx + 1
        idx = self.counts[dom]
        self.counts[dom] = idx + 1
        cvec = dict(ec)
        cvec[dom] = max(cvec.get(dom, 0), idx + 1)
        rec = _Op()
        rec.dom, rec.idx, rec.fn = dom, idx, fn
        rec.waits, rec.marked, rec.cvec, rec.tag = waits, False, cvec, tag
        self.ops[dom].append(rec)
        self.streams[eng].append(rec)
        ent = (dom, idx, cvec)
        if dom not in self.ENGS:
            self.last_dma[dom] = ent
        for r in reads:
            self._reg(r)
            self.res.setdefault(r, {"w": None, "r": []})["r"].append(ent)
        for w in writes:
            self._reg(w)
            e = self.res.setdefault(w, {"w": None, "r": []})
            e["w"] = ent
            e["r"] = []
            e.pop("init", None)
        for (ddom, didx) in waits:
            if ddom in self.ENGS:
                self.ops[ddom][didx].marked = True
        return rec

    def _reg(self, k):
        base = k[0] if (isinstance(k, tuple) and isinstance(k[0], tuple)) else k
        self.derived = getattr(self, "derived", {})
        self.derived.setdefault(base, set()).add(k)

    def seed_init(self, key, dep_ents):
        if dep_ents:
            self.res[key] = {"w": None, "r": [], "init": list(dep_ents)}

    def final_ents(self, key):
        self.derived = getattr(self, "derived", {})
        out = []
        for k in ({key} | self.derived.get(key, set())):
            e = self.res.get(k)
            if not e:
                continue
            if e["w"] is not None:
                out.append(e["w"])
            out.extend(e["r"])
            out.extend(e.get("init", ()))
        return out

    def finalize(self):
        waits = []
        for d in self.ENGS + self.qdoms:
            n = self.counts[d]
            if n > 0 and d != "SP":
                waits.append((d, n - 1))
                if d in self.ENGS:
                    self.ops[d][n - 1].marked = True
        rec = _Op()
        rec.dom, rec.idx, rec.fn = "SP", self.counts["SP"], (lambda e: e.nop())
        rec.waits, rec.marked, rec.cvec, rec.tag = waits, False, {}, "finalbar"
        self.counts["SP"] += 1
        self.ops["SP"].append(rec)
        self.streams["SP"].append(rec)
        self._pref = {}
        for d in self.ENGS:
            cnt, pref = 0, []
            for r in self.ops[d]:
                if r.marked:
                    cnt += 1
                pref.append(cnt)
            self._pref[d] = pref

    def emit(self, eng, engobj):
        pref = self._pref
        for rec in self.streams[eng]:
            for (ddom, didx) in rec.waits:
                val = pref[ddom][didx] if ddom in self.ENGS else 16 * (didx + 1)
                engobj.wait_ge(self.sems[ddom], val)
            inst = rec.fn(engobj)
            if rec.dom not in self.ENGS:
                inst.then_inc(self.sems[rec.dom], 16)
            elif rec.marked:
                inst.then_inc(self.sems[rec.dom], 1)


class SB:
    __slots__ = ("h", "key")

    def __init__(self, h, key):
        self.h = h
        self.key = key

    def __getitem__(self, sl):
        return self.h[sl]

    @property
    def ap(self):
        return self.h[:]


class _Row:
    __slots__ = ("b", "i", "pi", "key", "lkey")

    def __init__(self, b, i):
        self.b, self.i, self.pi = b, i, 64 * i
        self.key = (b.key, i)
        self.lkey = (b.key, i, "l")

    @property
    def ap(self):
        return self.b.h[self.pi:self.pi + 1, :]


class Arena:
    def __init__(self, nc, prog, base, top):
        self.nc, self.prog = nc, prog
        self.base, self.top = base, top
        self.cur = base
        self.live = {}
        self.dead = []
        self.marks = []
        self.uid = 0
        self.peak = base

    def alloc(self, name, shape, dtype):
        assert len(shape) == 2 and shape[0] <= 128
        bpp = shape[1] * DT_SIZE[dtype]
        self.uid += 1
        guard = self.nc.sbuf_tensor(f"{name}_{self.uid}", list(shape), dtype)
        h = guard.__enter__()
        mloc = self.nc.lookup_mloc(h)
        off = mloc.addr
        self.cur = max(self.cur, off + bpp)
        self.peak = max(self.peak, self.cur)
        key = (name, self.uid)
        init = []
        for i in range(len(self.dead) - 1, -1, -1):
            s, e, k2 = self.dead[i]
            if s < off + bpp and e > off:
                init.extend(self.prog.final_ents(k2))
        if init:
            self.prog.seed_init(key, init)
        self.live[key] = (off, off + bpp, guard)
        self.order = getattr(self, "order", [])
        self.order.append(key)
        return SB(h, key)

    def mark(self):
        self.marks.append(set(self.live))

    def release(self):
        keep = self.marks.pop()
        self.order = getattr(self, "order", [])
        for k in reversed(self.order):
            if k in self.live and k not in keep:
                s, e, guard = self.live[k]
                self.dead.append((s, e, k))
                del self.live[k]
                guard.__exit__(None, None, None)
        self.order = [k for k in self.order if k in self.live]


class Build:
    def __init__(self, dbg=(), phases=ALL_PHASES, moe_experts=None):
        self.dbg = set(dbg)
        self.phases = set(phases)
        self.moe_experts = list(range(1 + NE)) if moe_experts is None else list(moe_experts)
        self.nc = bass.Bass("TRN2", target_bir_lowering=False)
        self.p = Prog()
        self.consts = _host_consts()
        self.dbg_tensors = {}
        self._mkio()

    def _mkio(self):
        nc = self.nc
        self.xin = nc.dram_tensor("xin", [T, D], F32, kind="ExternalInput").ap()
        wshapes = {
            "w_qkv": [3 * D, D], "w_conv": [D, DH * CF], "b_conv": [D],
            "null_k": [D], "null_v": [D], "w_out": [D, D], "b_out": [D],
            "ln1_w": [D], "ln2_w": [D], "kv_d": [LAT, D], "q_d": [LAT, D],
            "k_u": [D, LAT], "q_u": [D, LAT], "v_u": [D, LAT],
            "rope_q": [NH * RD, LAT], "rope_k": [NH * RD, D],
            "o_w": [D, D], "router_w": [NE, D], "routing_bias": [NE],
            "sh_gate": [IE, D], "sh_up": [IE, D], "sh_down": [D, IE],
            "r_gate": [NE, IE, D], "r_up": [NE, IE, D], "r_down": [NE, D, IE],
        }
        self.win = {k: nc.dram_tensor(k, v, F32, kind="ExternalInput").ap()
                    for k, v in wshapes.items()}
        self.out = nc.dram_tensor("out", [T, D], F32, kind="ExternalOutput").ap()
        self.cst = {k: nc.inline_tensor(np.ascontiguousarray(self.consts[k]),
                                        name=k).ap()
                    for k in ("ident_bf", "ident_f32", "cosT", "sinT", "Rmat_bf",
                              "mask_diag_bf")}
        self.cst_om = {k: nc.inline_tensor(np.ascontiguousarray(v),
                                           name=f"om{k[0]}_{k[1]}").ap()
                       for k, v in self.consts["outer_masks"].items()}
        dram = lambda n, s, d: nc.dram_tensor(n, s, d, kind="Internal").ap()
        self.dr = {}
        for n, s, d in [
            ("wqkv_bf", [3 * D, D], BF16), ("wconv_bf", [D, DH * CF], BF16),
            ("wout_bf", [D, D], BF16), ("kvd_bf", [LAT, D], BF16),
            ("qd_bf", [LAT, D], BF16), ("ku_bf", [D, LAT], BF16),
            ("qu_bf", [D, LAT], BF16), ("vu_bf", [D, LAT], BF16),
            ("ropeq_bf", [NH * RD, LAT], BF16), ("ropek_bf", [NH * RD, D], BF16),
            ("ow_bf", [D, D], BF16), ("router_bf", [NE, D], BF16),
            ("sg_bf", [IE, D], BF16), ("su_bf", [IE, D], BF16),
            ("sd_bf", [D, IE], BF16), ("rg_bf", [NE, IE, D], BF16),
            ("ru_bf", [NE, IE, D], BF16), ("rd_bf", [NE, D, IE], BF16),
            ("vraw", [NCH, 128, T], BF16), ("qouter", [NCH, 128, T], BF16),
            ("cat_o", [NCH, 128, T], BF16), ("cat_i", [NCH, 128, T], BF16),
            ("x1", [T, D], F32), ("y0", [T, D], F32),
            ("kfull", [NCH, 128, T], BF16), ("qfull", [NCH, 128, T], BF16),
            ("vti", [T // 128, 128, NH * 65], BF16),
            ("xtf", [NCH, 128, T], BF16), ("wgt", [NE, T], F32),
            ("wgt_tm", [T // 128, 128, NE], F32),
        ]:
            self.dr[n] = dram(n, s, d)

    def dbg_out(self, name, shape, dtype):
        t = self.nc.dram_tensor(f"dbg_{name}", shape, dtype,
                                kind="ExternalOutput").ap()
        self.dbg_tensors[name] = t
        return t

    def dbg_dump_dram(self, name, src, reads):
        if name not in self.dbg:
            return
        t = self.dbg_out(name, list(src.shape), src.dtype)
        self.p.op("QSP", (lambda e: e.dma_start(out=t, in_=src)), reads=list(reads),
                  tag=f"dbg_{name}")

    def build(self):
        nc, p = self.nc, self.p
        with contextlib.ExitStack() as ctx:
            for d in Prog.ENGS + p.qdoms:
                p.sems[d] = ctx.enter_context(
                    nc.semaphore(f"s{d}".replace("#", "_")))
            self.pb = []
            for i in range(6):
                self.pb.append(ctx.enter_context(
                    nc.psum_tensor(f"pb{i}", [128, 512], F32)))
            for i in range(6, 8):
                self.pb.append(ctx.enter_context(
                    nc.psum_tensor(f"pb{i}", [128, 1024], BF16)))
            self.ar = Arena(nc, p, nc.sbuf_base, nc.sbuf_top)
            self._mk_small()
            self._stage_weights()
            ar = self.ar

            def _drive(gens):
                for g, _n, _l in gens:
                    next(g)
                maxlag = max(l for _g, _n, l in gens) if gens else 0
                for i in range(NTT + maxlag):
                    act = [j for j, (_g, _n, l) in enumerate(gens)
                           if 0 <= i - l < NTT]
                    cnt = {j: 0 for j in act}
                    while any(cnt[j] < gens[j][1] for j in act):
                        best, bfrac = -1, 2.0
                        for j in act:
                            if cnt[j] < gens[j][1]:
                                fr = cnt[j] / gens[j][1]
                                if fr < bfrac:
                                    best, bfrac = j, fr
                        next(gens[best][0])
                        cnt[best] += 1
                for g, _n, _l in gens:
                    for _ in g:
                        pass

            k_c = v_cT = None
            if "A" in self.phases or "A2" in self.phases or "B" in self.phases:
                k_c, v_cT = self._outer_attn_prep()
            if k_c is not None:
                self._emit_late_casts([("qouter",)])
            if "B" in self.phases or "CD" in self.phases:
                self._stage_moe_casts("shared", [("qouter",)])
                self._stage_moe_casts(0, [("qouter",)])
                self._stage_moe_casts(1, [("qouter",)])
                ar.mark()
                gens = []
                if "B" in self.phases:
                    gens.append((self.phase_B_iter(k_c, v_cT), NCH, 0))
                if "CD" in self.phases:
                    gens.append((self.phase_CD_iter(), 9, 1))
                _drive(gens)
                ar.release()
            if k_c is not None:
                ar.release()
            if "E" in self.phases or "F" in self.phases:
                for ee in (2, 3, 4, 5, 6):
                    self._stage_moe_casts(ee, [("x1",)])
                ar.mark()
                gens = []
                if "E" in self.phases:
                    gens.append((self.phase_E_iter(), NCH, 0))
                if "F" in self.phases:
                    gens.append((self.phase_F_iter(), 5, 1))
                _drive(gens)
                ar.release()
            if "G" in self.phases:
                self.phase_G()
            p.finalize()
            with nc.Block() as block:
                @block.tensor
                def _(eng):
                    p.emit("PE", eng)

                @block.vector
                def _(eng):
                    p.emit("DVE", eng)

                @block.scalar
                def _(eng):
                    p.emit("ACT", eng)

                @block.gpsimd
                def _(eng):
                    p.emit("POOL", eng)

                @block.sync
                def _(eng):
                    p.emit("SP", eng)
        return nc

    def _mk_small(self):
        p, ar = self.p, self.ar
        self.ident = ar.alloc("ident", [128, 128], BF16)
        self.ident32 = ar.alloc("ident32", [128, 128], F32)
        self.rmat = ar.alloc("rmat", [128, 128], BF16)
        self.mdiag = ar.alloc("mdiag", [128, 128], BF16)
        self.eps_t = ar.alloc("eps_t", [128, 1], F32)
        self.sc18 = ar.alloc("sc18", [128, 1], F32)
        self.sc132 = ar.alloc("sc132", [128, 1], F32)
        self.onesbf = ar.alloc("onesbf", [1, 128], BF16)
        self.onesk1 = ar.alloc("onesk1", [128, 128], F32)
        self.invD = ar.alloc("invD", [128, 1], F32)
        for dst, src in [(self.ident, self.cst["ident_bf"]),
                         (self.ident32, self.cst["ident_f32"]),
                         (self.rmat, self.cst["Rmat_bf"]),
                         (self.mdiag, self.cst["mask_diag_bf"])]:
            p.op("QSP", (lambda e, a=dst.ap, b=src: e.dma_start(out=a, in_=b)),
                 writes=[dst.key])
        for t, v in [(self.eps_t, EPS), (self.sc18, 0.125), (self.sc132, 1 / 32.0),
                     (self.onesbf, 1.0), (self.onesk1, 1.0),
                     (self.invD, 1.0 / D)]:
            p.op("DVE", (lambda e, a=t.ap, v=v: e.memset(a, v)), writes=[t.key])

    def _stage_weights(self):
        p, w, d = self.p, self.win, self.dr
        casts = [("wqkv_bf", w["w_qkv"]), ("wconv_bf", w["w_conv"]),
                 ("wout_bf", w["w_out"]), ("kvd_bf", w["kv_d"]),
                 ("qd_bf", w["q_d"]), ("ku_bf", w["k_u"]), ("qu_bf", w["q_u"]),
                 ("vu_bf", w["v_u"]), ("ropeq_bf", w["rope_q"]),
                 ("ropek_bf", w["rope_k"]), ("ow_bf", w["o_w"]),
                 ("router_bf", w["router_w"]), ("sg_bf", w["sh_gate"]),
                 ("su_bf", w["sh_up"]), ("sd_bf", w["sh_down"])]
        need = {"A": ["wqkv_bf", "wconv_bf"], "A2": ["wconv_bf"],
                "CD": ["wout_bf", "kvd_bf", "qd_bf", "ku_bf", "qu_bf", "vu_bf",
                       "ropeq_bf", "ropek_bf"],
                "F": ["ow_bf", "router_bf"],
                "G": ["sg_bf", "su_bf", "sd_bf", "rg_bf", "ru_bf", "rd_bf"]}
        wanted = set()
        for ph in self.phases:
            wanted.update(need.get(ph, ()))
        self._late_casts = []
        for nm, src in casts:
            if nm not in wanted or nm in ("sg_bf", "su_bf", "sd_bf"):
                continue
            if nm in ("wqkv_bf", "wconv_bf"):
                p.op("QPOOL", (lambda e, a=d[nm], b=src: e.dma_start(out=a,
                                                                    in_=b)),
                     writes=[("cast", nm)], tag=f"cast_{nm}")
            else:
                self._late_casts.append((nm, src))

    def _emit_late_casts(self, gate_reads):
        p, d = self.p, self.dr
        for nm, src in getattr(self, "_late_casts", []):
            p.op("QPOOL", (lambda e, a=d[nm], b=src: e.dma_start(out=a, in_=b)),
                 reads=list(gate_reads), writes=[("cast", nm)],
                 tag=f"cast_{nm}")
        self._late_casts = []

    def _stage_moe_casts(self, group, gate_reads):
        p, w, d = self.p, self.win, self.dr
        if "G" not in self.phases:
            return
        if group == "shared":
            for nm, src in [("sg_bf", w["sh_gate"]), ("su_bf", w["sh_up"]),
                            ("sd_bf", w["sh_down"])]:
                p.op("QPOOL", (lambda e, a=d[nm], b=src: e.dma_start(out=a,
                                                                    in_=b)),
                     reads=list(gate_reads), writes=[("cast", nm)],
                     tag=f"cast_{nm}")
        else:
            ee = group
            for nm, src in [("rg_bf", w["r_gate"]), ("ru_bf", w["r_up"]),
                            ("rd_bf", w["r_down"])]:
                p.op("QPOOL", (lambda e, a=d[nm][ee], b=src[ee]:
                               e.dma_start(out=a, in_=b)),
                     reads=list(gate_reads), writes=[("cast", nm, ee)],
                     tag=f"cast_{nm}{ee}")

    def wload_T(self, sb, src2d, castkey, col0=0, ncols=None, queue="QACT"):
        dst = sb.h[0:(src2d.shape[1]), col0:col0 + src2d.shape[0]] \
            if False else sb.h[:, col0:col0 + src2d.shape[0]]
        self.p.op(queue, (lambda e, a=dst, b=src2d: e.dma_start_transpose(a, b)),
                  reads=[castkey], writes=[(sb.key, "ld", col0)], tag="wT")

    def _ln_tok(self, xb, xkey, lnb, lnkey, sc, wmul=None, tag="ln"):
        p = self.p
        sq, s1, s2, mean, m2t, var, rstd = sc
        p.op("DVE", (lambda e: e.scalar_tensor_tensor(
            sq.ap, xb, 1.0, xb, ALU.mult, ALU.mult, accum_out=s2.ap)),
             reads=[xkey], writes=[sq.key, s2.key], tag=tag + "sq")
        p.op("DVE", (lambda e: e.tensor_scalar(
            sq.ap, xb, 1.0, 0.0, ALU.mult, ALU.add, accum_out=s1.ap)),
             reads=[xkey], writes=[sq.key, s1.key], tag=tag + "s1")
        p.op("DVE", (lambda e: e.tensor_scalar_mul(mean.ap, s1.ap, 1.0 / D)),
             reads=[s1.key], writes=[mean.key])
        p.op("DVE", (lambda e: e.scalar_tensor_tensor(
            var.ap, s1.ap, -1.0 / D, s1.ap, ALU.mult, ALU.mult)),
             reads=[s1.key], writes=[(var.key, "t")], tag=tag + "t")
        p.op("DVE", (lambda e: e.tensor_add(var.ap, var.ap, s2.ap)),
             reads=[(var.key, "t"), s2.key], writes=[var.key], tag=tag + "u")
        p.op("ACT", (lambda e: e.activation(rstd.ap, var.ap, AF.Sqrt,
                                            bias=self.eps_t.ap,
                                            scale=self.invD.ap)),
             reads=[var.key, self.eps_t.key, self.invD.key],
             writes=[(rstd.key, "s")])
        p.op("DVE", (lambda e: e.reciprocal(rstd.ap, rstd.ap)),
             reads=[(rstd.key, "s")], writes=[rstd.key])
        if wmul is None:
            p.op("DVE", (lambda e: e.tensor_scalar(
                lnb, xb, mean.ap, rstd.ap, ALU.subtract, ALU.mult)),
                 reads=[xkey, mean.key, rstd.key], writes=[lnkey], tag=tag + "ap")
        else:
            tmp, tkey = wmul[1], wmul[1].key
            p.op("DVE", (lambda e: e.tensor_scalar(
                tmp.ap, xb, mean.ap, rstd.ap, ALU.subtract, ALU.mult)),
                 reads=[xkey, mean.key, rstd.key], writes=[tkey], tag=tag + "ap")
            p.op("DVE", (lambda e: e.tensor_mul(lnb, tmp.ap, wmul[0].ap)),
                 reads=[tkey, wmul[0].key], writes=[lnkey], tag=tag + "wm")

    def _mk_ln_scratch(self, ar, pref, sqbuf=None):
        return (sqbuf if sqbuf is not None else ar.alloc(pref + "sq", [128, D], F32),
                ar.alloc(pref + "s1", [128, 1], F32),
                ar.alloc(pref + "s2", [128, 1], F32),
                ar.alloc(pref + "mean", [128, 1], F32),
                ar.alloc(pref + "m2t", [128, 1], F32),
                ar.alloc(pref + "var", [128, 1], F32),
                ar.alloc(pref + "rstd", [128, 1], F32))

    def _transpose_tok_to_f(self, lnb, lnkey, dsts, dcol, tag="T"):
        p = self.p
        for half in range(2):
            bank = 6 + half
            for cc in range(4):
                c = half * 4 + cc
                p.op("PE", (lambda e, b=self.pb[bank], cc=cc, c=c:
                            e.transpose(b[:, cc * 128:(cc + 1) * 128],
                                        lnb[:, c * 128:(c + 1) * 128],
                                        self.ident.ap)),
                     reads=[lnkey, self.ident.key], writes=[f"PS{bank}"],
                     tag=tag + "t")
            eng = "DVE" if half == 0 else "ACT"
            for cc in range(4):
                c = half * 4 + cc
                p.op(eng, (lambda e, b=self.pb[bank], cc=cc, c=c:
                           (e.tensor_copy if hasattr(e, "tensor_copy") else e.copy)(
                               dsts[c].h[:, dcol:dcol + 128],
                               b[:, cc * 128:(cc + 1) * 128])),
                     reads=[f"PS{bank}"], writes=[(dsts[c].key, "q", dcol)],
                     tag=tag + "e")

    def _outer_attn_prep(self):
        nc, p, ar = self.nc, self.p, self.ar
        ar.mark()
        k_c = [ar.alloc(f"kc{c}", [128, JTOT], BF16) for c in range(NCH)]
        v_cT = [ar.alloc(f"vcT{j}", [128, NH * 65], BF16) for j in range(NJT)]
        ar.mark()
        wcT = [ar.alloc(f"wcT{t}", [128, D], BF16) for t in range(CF)]
        bconv = ar.alloc("bconv", [128, NCH], F32)
        bconv_row = ar.alloc("bconvrow", [1, D], F32)
        ckc = ("cast", "wconv_bf")
        wcnat = ar.alloc("wcnat", [128, DH * CF], BF16)
        for c in range(NCH):
            p.op("QACT", (lambda e, a=wcnat.ap,
                          b=self.dr["wconv_bf"][c * 128:(c + 1) * 128, :]:
                          e.dma_start(out=a, in_=b)),
                 reads=[ckc], writes=[wcnat.key], tag="wcnat")
            for t in range(CF):
                bank = 6 + (c * CF + t) % 2
                p.op("PE", (lambda e, b=self.pb[bank], t=t:
                            e.transpose(b[0:64, 0:128],
                                        wcnat.h[:, t:DH * CF:CF], self.ident.ap)),
                     reads=[wcnat.key, self.ident.key], writes=[f"PS{bank}"])
                p.op("DVE", (lambda e, b=self.pb[bank], t=t, c=c:
                             e.tensor_copy(wcT[t].h[0:64, c * 128:(c + 1) * 128],
                                           b[0:64, 0:128])),
                     reads=[f"PS{bank}"], writes=[wcT[t].key])
                p.op("DVE", (lambda e, b=self.pb[bank], t=t, c=c:
                             e.tensor_copy(wcT[t].h[64:128, c * 128:(c + 1) * 128],
                                           b[0:64, 0:128])),
                     reads=[f"PS{bank}"], writes=[wcT[t].key])
        p.op("QSP", (lambda e: e.dma_start(
            out=bconv.ap, in_=self.win["b_conv"].rearrange("(c p) -> p c", p=128),
            allow_slow_non_contiguous=True)), writes=[bconv.key])
        p.op("QSP", (lambda e: e.dma_start(
            out=bconv_row.ap, in_=self.win["b_conv"].rearrange("(a d) -> a d", a=1))),
             writes=[bconv_row.key])
        for c in range(NCH):
            p.op("QPOOL", (lambda e, c=c: e.dma_start(
                out=k_c[c].h[:, JTOT - 1:JTOT],
                in_=self.win["null_k"][c * 128:(c + 1) * 128].rearrange("(p a) -> p a", a=1),
                allow_slow_non_contiguous=True)),
                 writes=[(k_c[c].key, "null")], tag="nullk")
        for j in range(NJT):
            p.op("DVE", (lambda e, j=j: e.memset(v_cT[j].ap, 0.0)),
                 writes=[(v_cT[j].key, "zero")])
            p.op("DVE", (lambda e, j=j: e.memset(
                v_cT[j].h[:, 64:NH * 65:65], 1.0)),
                 reads=[(v_cT[j].key, "zero")], writes=[(v_cT[j].key, "ones")])
        p.op("QPOOL", (lambda e: e.dma_start(
            out=v_cT[10].h[86:87, :].rearrange("p (h x) -> p h x", x=65)[:, :, 0:64],
            in_=self.win["null_v"].rearrange("(a h x) -> a h x", a=1, x=64),
            allow_slow_non_contiguous=True)),
             reads=[(v_cT[10].key, "zero"), (v_cT[10].key, "ones")],
             writes=[(v_cT[10].key, "null")], tag="nullv")

        if "A" in self.phases:
            self.phase_A(k_c, wcT, bconv)
        if "A2" in self.phases:
            self.phase_A2(v_cT, wcT, bconv_row)
        ar.release()
        return k_c, v_cT

    def phase_A(self, k_c, wcT, bconv):
        nc, p, ar = self.nc, self.p, self.ar
        ar.mark()
        wqkvT = [ar.alloc(f"wqkvT{c}", [128, 3 * D], BF16) for c in range(NCH)]
        ck = ("cast", "wqkv_bf")
        for c in range(NCH):
            p.op("QACT", (lambda e, a=wqkvT[c].ap,
                          b=self.dr["wqkv_bf"][:, c * 128:(c + 1) * 128]:
                          e.dma_start_transpose(a, b)),
                 reads=[ck], writes=[wqkvT[c].key], tag="wqkvT")
        x_tok = [ar.alloc(f"xtok{i}", [128, D], F32) for i in range(4)]
        lnsc2 = [self._mk_ln_scratch(ar, f"a{i}") for i in range(2)]
        ln_tok = [ar.alloc(f"lntok{i}", [128, D], BF16) for i in range(4)]
        ln_f2 = [[ar.alloc(f"lnf{pp}_{c}", [128, TT], BF16) for c in range(NCH)]
                 for pp in range(2)]
        kraw = [ar.alloc(f"kraw{c}", [128, 2 + TT], BF16) for c in range(NCH)]
        vraw_t = [ar.alloc(f"vrawt{c}", [128, TT], BF16) for c in range(NCH)]
        qtile = [ar.alloc(f"qt{c}", [128, TT], BF16) for c in range(NCH)]
        for c in range(NCH):
            p.op("DVE", (lambda e, c=c: e.memset(kraw[c].h[:, 0:2], 0.0)),
                 writes=[(kraw[c].key, "hdr")], tag="kpad0")

        def _qkv_conv(tt):
            ln_f = ln_f2[tt % 2]
            lnf_reads = [(ln_f[c].key, "q", s * 128)
                         for c in range(NCH) for s in range(4)]
            for m in range(24):
                bank = m % 4
                for k in range(NCH):
                    p.op("PE", (lambda e, b=self.pb[bank], k=k, m=m,
                                w=wqkvT[k], lf=ln_f[k]:
                                e.matmul(b[:, 0:TT],
                                         w.h[:, m * 128:(m + 1) * 128],
                                         lf.ap, start=(k == 0),
                                         stop=(k == NCH - 1))),
                         reads=[wqkvT[k].key] + lnf_reads, writes=[f"PS{bank}"],
                         tag="qkv")
                if m < 8:
                    dst, dkey = qtile[m].ap, qtile[m].key
                elif m < 16:
                    dst, dkey = kraw[m - 8].h[:, 2:2 + TT], (kraw[m - 8].key, "body")
                else:
                    dst, dkey = vraw_t[m - 16].ap, vraw_t[m - 16].key
                p.op("ACT", (lambda e, b=self.pb[bank], d=dst:
                             e.copy(d, b[:, 0:TT])),
                     reads=[f"PS{bank}"], writes=[dkey], tag="qkv_ev")
            for c in range(NCH):
                p.op("QACT", (lambda e, c=c, tt=tt: e.dma_start(
                    out=self.dr["qouter"][c, :, tt * TT:(tt + 1) * TT],
                    in_=qtile[c].ap)),
                     reads=[qtile[c].key], writes=[("qouter",)], tag="qspill")
                p.op("QACT", (lambda e, c=c, tt=tt: e.dma_start(
                    out=self.dr["vraw"][c, :, tt * TT:(tt + 1) * TT],
                    in_=vraw_t[c].ap)),
                     reads=[vraw_t[c].key], writes=[("vraw",)], tag="vspill")
            j0 = -(-512 * tt // 3)
            j1 = (512 * tt + 511) // 3
            nj = j1 - j0 + 1
            for g in range(NH):
                c, hh = g // 2, (g % 2) * 64
                bank = 4 + g % 2
                for t in range(CF):
                    off = 3 * j0 + t - 512 * tt
                    p.op("PE", (lambda e, b=self.pb[bank], c=c, hh=hh, t=t,
                                off=off, nj=nj, g=g:
                                e.matmul(b[0:64, 0:nj],
                                         wcT[t].h[hh:hh + 64, g * 64:g * 64 + 64],
                                         kraw[c].h[hh:hh + 64, off:off + 3 * (nj - 1) + 1:3],
                                         start=(t == 0), stop=(t == CF - 1))),
                         reads=[wcT[t].key, (kraw[c].key, "body"),
                                (kraw[c].key, "hdr")],
                         writes=[f"PS{bank}"], tag="kconv")
                p.op("DVE", (lambda e, b=self.pb[bank], c=c, hh=hh, j0=j0, nj=nj:
                             e.tensor_scalar_add(k_c[c].h[hh:hh + 64, j0:j0 + nj],
                                                 b[0:64, 0:nj],
                                                 bconv.h[hh:hh + 64, c:c + 1])),
                     reads=[f"PS{bank}", bconv.key], writes=[(k_c[c].key, "cv")],
                     tag="kconv_ev")
            if tt < NTT - 1:
                for c in range(NCH):
                    p.op("DVE", (lambda e, c=c: e.tensor_copy(
                        kraw[c].h[:, 0:2], kraw[c].h[:, TT:TT + 2])),
                         reads=[(kraw[c].key, "body")],
                         writes=[(kraw[c].key, "hdr")], tag="kroll")

        for tt in range(NTT):
            for s in range(4):
                xb, lnb = x_tok[s], ln_tok[s]
                t0 = tt * TT + s * 128
                p.op("QSP", (lambda e, a=xb.ap, b=self.xin[t0:t0 + 128, :]:
                             e.dma_start(out=a, in_=b)),
                     writes=[xb.key], tag="xload")
                self._ln_tok(xb.ap, xb.key, lnb.ap, lnb.key, lnsc2[s % 2],
                             tag="ln0")
            if tt > 0:
                _qkv_conv(tt - 1)
            for s in range(4):
                self._transpose_tok_to_f(ln_tok[s], ln_tok[s].key,
                                         ln_f2[tt % 2], s * 128, tag="ln0T")
        _qkv_conv(NTT - 1)
        if "k_c" in self.dbg:
            t = self.dbg_out("k_c", [NCH, 128, JTOT], BF16)
            for c in range(NCH):
                p.op("QSP", (lambda e, c=c, t=t: e.dma_start(out=t[c],
                                                             in_=k_c[c].ap)),
                     reads=[(k_c[c].key, "cv"), (k_c[c].key, "null")], tag="dbgkc")
        self.dbg_dump_dram("q_f", self.dr["qouter"], [("qouter",)])
        self.dbg_dump_dram("vraw", self.dr["vraw"], [("vraw",)])
        ar.release()

    def phase_A2(self, v_cT, wcT, bconv_row):
        nc, p, ar = self.nc, self.p, self.ar
        ar.mark()
        vpad = [ar.alloc(f"vpad{c}", [128, 392], BF16) for c in range(NCH)]
        for jt in range(NJT):
            j_hi = min(jt * 128 + 127, NCONV - 1)
            njj = j_hi - jt * 128 + 1
            lo_pad = 384 * jt
            ln = 3 * j_hi + 2 - lo_pad + 1
            for c in range(NCH):
                if jt == 0:
                    p.op("DVE", (lambda e, c=c: e.memset(vpad[c].h[:, 0:2], 0.0)),
                         writes=[(vpad[c].key, "h")])
                    p.op("QSP", (lambda e, c=c, ln=ln: e.dma_start(
                        out=vpad[c].h[:, 2:ln],
                        in_=self.dr["vraw"][c, :, 0:ln - 2])),
                         reads=[("vraw",)], writes=[vpad[c].key], tag="vpadld")
                else:
                    p.op("QSP", (lambda e, c=c, lo=lo_pad - 2, ln=ln: e.dma_start(
                        out=vpad[c].h[:, 0:ln],
                        in_=self.dr["vraw"][c, :, lo:lo + ln])),
                         reads=[("vraw",)], writes=[vpad[c].key], tag="vpadld")
            for g in range(NH):
                c, hh = g // 2, (g % 2) * 64
                bank = g % 2
                for t in range(CF):
                    p.op("PE", (lambda e, b=self.pb[bank], c=c, hh=hh, t=t,
                                njj=njj, g=g:
                                e.matmul(b[0:njj, 0:64],
                                         vpad[c].h[hh:hh + 64, t:t + 3 * (njj - 1) + 1:3],
                                         wcT[t].h[hh:hh + 64, g * 64:g * 64 + 64],
                                         start=(t == 0), stop=False)),
                         reads=[wcT[t].key, vpad[c].key, (vpad[c].key, "h")],
                         writes=[f"PS{bank}"], tag="vconv")
                p.op("PE", (lambda e, b=self.pb[bank], njj=njj, g=g:
                            e.matmul(b[0:njj, 0:64],
                                     self.onesk1.h[0:1, 0:njj],
                                     bconv_row.h[0:1, g * 64:g * 64 + 64],
                                     start=False, stop=True)),
                     reads=[self.onesk1.key, bconv_row.key],
                     writes=[f"PS{bank}"], tag="vconv_b")
                p.op("DVE", (lambda e, b=self.pb[bank], jt=jt, g=g, njj=njj:
                             e.tensor_copy(
                                 v_cT[jt].h[0:njj, g * 65:g * 65 + 64],
                                 b[0:njj, 0:64])),
                     reads=[f"PS{bank}", (v_cT[jt].key, "zero"),
                            (v_cT[jt].key, "ones")],
                     writes=[(v_cT[jt].key, "cv")], tag="vconv_ev")
        if "v_cT" in self.dbg:
            t = self.dbg_out("v_cT", [NJT, 128, NH * 65], BF16)
            for j in range(NJT):
                p.op("QSP", (lambda e, j=j, t=t: e.dma_start(out=t[j],
                                                             in_=v_cT[j].ap)),
                     reads=[(v_cT[j].key, "cv"), (v_cT[j].key, "ones"),
                            (v_cT[j].key, "null")], tag="dbgvc")
        ar.release()

    def phase_B_iter(self, k_c, v_cT):
        nc, p, ar = self.nc, self.p, self.ar
        kind = self.consts["outer_kind"]
        mbufs = [ar.alloc(f"om{i}", [128, TT], BF16) for i in range(3)]
        masks = {}

        def _load_masks(qg):
            need = [jt for jt in range(NJT) if kind[jt, qg] == 2]
            masks.clear()
            assert len(need) <= 3
            for i, jt in enumerate(need):
                m = mbufs[i % 3]
                p.op("QSP", (lambda e, a=m.ap, b=self.cst_om[(jt, qg)]:
                             e.dma_start(out=a, in_=b)),
                     writes=[m.key], tag="maskld")
                masks[(jt, qg)] = m
        qch = [ar.alloc(f"bq{i}", [128, TT], BF16) for i in range(2)]
        ptile = [ar.alloc(f"bp{i}", [128, TT], BF16) for i in range(6)]
        catb = [ar.alloc(f"bcat{i}", [64, TT], BF16) for i in range(4)]
        lrow4 = [ar.alloc(f"blr4{j}", [128, TT], F32) for j in range(2)]
        lrow = [_Row(lrow4[i // 2], i % 2) for i in range(4)]
        pend_tail = []
        yield
        for qg in range(NTT):
            _load_masks(qg)
            for c in range(NCH):
                qb = qch[c % 2]
                p.op("QSP", (lambda e, a=qb.ap, c=c, qg=qg: e.dma_start(
                    out=a, in_=self.dr["qouter"][c, :, qg * TT:(qg + 1) * TT])),
                     reads=[("qouter",)], writes=[qb.key], tag="bqld")
                jts = [jt for jt in range(NJT) if kind[jt, qg] != 0]

                def _qk(ji, jt, c=c, qb=qb, qg=qg):
                    mj = min(128, JTOT - jt * 128)
                    for half in range(2):
                        sb = 2 * (ji % 2) + half
                        hh = half * 64
                        pt = ptile[2 * (ji % 3) + half]
                        p.op("PE", (lambda e, b=self.pb[sb], c=c, hh=hh, jt=jt,
                                    mj=mj, qb=qb:
                                    e.matmul(b[0:mj, 0:TT],
                                             k_c[c].h[hh:hh + 64,
                                                      jt * 128:jt * 128 + mj],
                                             qb.h[hh:hh + 64, :],
                                             start=True, stop=True)),
                             reads=[(k_c[c].key, "cv"), (k_c[c].key, "null"),
                                    qb.key],
                             writes=[f"PS{sb}"], tag="oqk")
                        p.op("ACT", (lambda e, b=self.pb[sb], pt=pt, mj=mj:
                                     e.activation(pt.h[0:mj, :], b[0:mj, 0:TT],
                                                  AF.Exp,
                                                  scale=self.sc132.h[0:mj, :])),
                             reads=[f"PS{sb}", self.sc132.key], writes=[pt.key],
                             tag="oexp")
                        if kind[jt, qg] == 2:
                            mk = masks[(jt, qg)]
                            p.op("DVE", (lambda e, pt=pt, mk=mk, mj=mj:
                                         e.tensor_mul(pt.h[0:mj, :],
                                                      pt.h[0:mj, :],
                                                      mk.h[0:mj, :])),
                                 reads=[mk.key, pt.key], writes=[pt.key],
                                 tag="omask")

                def _av(ji, jt, c=c):
                    mj = min(128, JTOT - jt * 128)
                    for half in range(2):
                        ab = 4 + half
                        h = 2 * c + half
                        pt = ptile[2 * (ji % 3) + half]
                        p.op("PE", (lambda e, b=self.pb[ab], jt=jt, h=h, mj=mj,
                                    pt=pt, first=(ji == 0),
                                    last=(jt == jts[-1]):
                                    e.matmul(b[0:65, 0:TT],
                                             v_cT[jt].h[0:mj, h * 65:h * 65 + 65],
                                             pt.h[0:mj, :],
                                             start=first, stop=last)),
                             reads=[(v_cT[jt].key, "cv"), (v_cT[jt].key, "ones"),
                                    (v_cT[jt].key, "null"), pt.key],
                             writes=[f"PS{ab}"], tag="oav")

                for ji in range(len(jts) + 2):
                    if ji < len(jts):
                        _qk(ji, jts[ji])
                    if ji == 3 and pend_tail:
                        for f in pend_tail:
                            f()
                        pend_tail = []
                    if ji >= 2:
                        _av(ji - 2, jts[ji - 2])
                for half in range(2):
                    ab = 4 + half
                    spb = 2 * (len(jts) % 2) + half
                    hh = half * 64
                    cb = catb[2 * (c % 2) + half]
                    lr = lrow[2 * (c % 2) + half]
                    p.op("ACT", (lambda e, b=self.pb[ab], cb=cb:
                                 e.copy(cb.ap, b[0:64, 0:TT])),
                         reads=[f"PS{ab}"], writes=[(cb.key, "raw")], tag="ocat")
                    p.op("DVE", (lambda e, b=self.pb[ab], lr=lr:
                                 e.tensor_copy(lr.ap, b[64:65, 0:TT])),
                         reads=[f"PS{ab}"], writes=[lr.lkey], tag="olcp")
                    p.op("DVE", (lambda e, lr=lr: e.reciprocal(lr.ap, lr.ap)),
                         reads=[lr.lkey], writes=[lr.key], tag="olinv")

                    def _tail(spb=spb, cb=cb, lr=lr, c=c, hh=hh, qg=qg):
                        p.op("PE", (lambda e, b=self.pb[spb], lr=lr:
                                    e.matmul(b[0:64, 0:TT],
                                             self.onesk1.h[lr.pi:lr.pi + 1,
                                                           0:64],
                                             lr.ap, start=True, stop=True)),
                             reads=[self.onesk1.key, lr.key],
                             writes=[f"PS{spb}"], tag="obc")
                        p.op("DVE", (lambda e, b=self.pb[spb], cb=cb:
                                     e.tensor_mul(cb.ap, cb.ap, b[0:64, 0:TT])),
                             reads=[f"PS{spb}", (cb.key, "raw")],
                             writes=[cb.key], tag="onorm")
                        p.op("QACT", (lambda e, cb=cb, c=c, hh=hh, qg=qg:
                                      e.dma_start(
                                          out=self.dr["cat_o"][c, hh:hh + 64,
                                                               qg * TT:
                                                               (qg + 1) * TT],
                                          in_=cb.ap)),
                             reads=[cb.key], writes=[("cat_o",)], tag="ocatst")
                    pend_tail.append(_tail)
                if c == NCH - 1:
                    for f in pend_tail:
                        f()
                    pend_tail = []
                yield
        self.dbg_dump_dram("cat_o", self.dr["cat_o"], [("cat_o",)])

    def phase_CD_iter(self):
        nc, p, ar = self.nc, self.p, self.ar
        w_outT = [ar.alloc(f"woT{k}", [128, D], BF16) for k in range(NCH)]
        for k in range(NCH):
            p.op("QACT", (lambda e, a=w_outT[k].ap,
                          b=self.dr["wout_bf"][:, k * 128:(k + 1) * 128]:
                          e.dma_start_transpose(a, b)),
                 reads=[("cast", "wout_bf")], writes=[w_outT[k].key], tag="woT")
        kv_dT = [ar.alloc(f"kvdT{k}", [128, LAT], BF16) for k in range(NCH)]
        q_dT = [ar.alloc(f"qdT{k}", [128, LAT], BF16) for k in range(NCH)]
        for k in range(NCH):
            p.op("QACT", (lambda e, a=kv_dT[k].ap,
                          b=self.dr["kvd_bf"][:, k * 128:(k + 1) * 128]:
                          e.dma_start_transpose(a, b)),
                 reads=[("cast", "kvd_bf")], writes=[kv_dT[k].key])
            p.op("QACT", (lambda e, a=q_dT[k].ap,
                          b=self.dr["qd_bf"][:, k * 128:(k + 1) * 128]:
                          e.dma_start_transpose(a, b)),
                 reads=[("cast", "qd_bf")], writes=[q_dT[k].key])
        k_uT = [ar.alloc(f"kuT{k}", [128, D], BF16) for k in range(2)]
        q_uT = [ar.alloc(f"quT{k}", [128, D], BF16) for k in range(2)]
        v_uT = [ar.alloc(f"vuT{k}", [128, D], BF16) for k in range(2)]
        rope_qT = [ar.alloc(f"rqT{k}", [128, NH * RD], BF16) for k in range(2)]
        rope_kT = [ar.alloc(f"rkT{k}", [128, NH * RD], BF16) for k in range(NCH)]
        for k in range(2):
            for sb, nm in [(k_uT[k], "ku_bf"), (q_uT[k], "qu_bf"),
                           (v_uT[k], "vu_bf")]:
                p.op("QACT", (lambda e, a=sb.ap,
                              b=self.dr[nm][:, k * 128:(k + 1) * 128]:
                              e.dma_start_transpose(a, b)),
                     reads=[("cast", nm)], writes=[sb.key])
            p.op("QACT", (lambda e, a=rope_qT[k].ap,
                          b=self.dr["ropeq_bf"][:, k * 128:(k + 1) * 128]:
                          e.dma_start_transpose(a, b)),
                 reads=[("cast", "ropeq_bf")], writes=[rope_qT[k].key])
        for k in range(NCH):
            p.op("QACT", (lambda e, a=rope_kT[k].ap,
                          b=self.dr["ropek_bf"][:, k * 128:(k + 1) * 128]:
                          e.dma_start_transpose(a, b)),
                 reads=[("cast", "ropek_bf")], writes=[rope_kT[k].key])
        KUSEL = [[ar.alloc(f"kus{c}_{k}", [128, 128], BF16) for k in range(2)]
                 for c in range(NCH)]
        RKSEL = [[ar.alloc(f"rks{c}_{k}", [128, 128], BF16) for k in range(NCH)]
                 for c in range(NCH)]
        QCOMB = [ar.alloc(f"qcmb{k}", [128, D], BF16) for k in range(2)]
        for c in range(NCH):
            for k in range(2):
                sbb = KUSEL[c][k]
                p.op("DVE", (lambda e, a=sbb.ap: e.memset(a, 0.0)),
                     writes=[(sbb.key, "z")])
                for hi in range(2):
                    p.op("DVE", (lambda e, sbb=sbb, hi=hi, c=c, k=k:
                                 e.tensor_copy(
                                     sbb.h[:, 64 * hi + 32:64 * hi + 64],
                                     k_uT[k].h[:, (2 * c + hi) * 64 + 32:
                                               (2 * c + hi) * 64 + 64])),
                         reads=[(sbb.key, "z"), k_uT[k].key], writes=[sbb.key])
            for k in range(NCH):
                sbb = RKSEL[c][k]
                p.op("DVE", (lambda e, a=sbb.ap: e.memset(a, 0.0)),
                     writes=[(sbb.key, "z")])
                for hi in range(2):
                    p.op("DVE", (lambda e, sbb=sbb, hi=hi, c=c, k=k:
                                 e.tensor_copy(
                                     sbb.h[:, 64 * hi:64 * hi + 32],
                                     rope_kT[k].h[:, (2 * c + hi) * 32:
                                                  (2 * c + hi) * 32 + 32])),
                         reads=[(sbb.key, "z"), rope_kT[k].key],
                         writes=[sbb.key])
        for k in range(2):
            sbb = QCOMB[k]
            for c in range(NCH):
                for hi in range(2):
                    p.op("DVE", (lambda e, sbb=sbb, hi=hi, c=c, k=k:
                                 e.tensor_copy(
                                     sbb.h[:, c * 128 + 64 * hi:
                                           c * 128 + 64 * hi + 32],
                                     rope_qT[k].h[:, (2 * c + hi) * 32:
                                                  (2 * c + hi) * 32 + 32])),
                         reads=[rope_qT[k].key], writes=[sbb.key])
                    p.op("DVE", (lambda e, sbb=sbb, hi=hi, c=c, k=k:
                                 e.tensor_copy(
                                     sbb.h[:, c * 128 + 64 * hi + 32:
                                           c * 128 + 64 * hi + 64],
                                     q_uT[k].h[:, (2 * c + hi) * 64 + 32:
                                               (2 * c + hi) * 64 + 64])),
                         reads=[q_uT[k].key], writes=[sbb.key])
        ln1row = ar.alloc("ln1row", [1, D], F32)
        boutrow = ar.alloc("boutrow", [1, D], F32)
        p.op("QSP", (lambda e: e.dma_start(
            out=ln1row.ap, in_=self.win["ln1_w"].rearrange("(a d) -> a d", a=1))),
             writes=[ln1row.key])
        p.op("QSP", (lambda e: e.dma_start(
            out=boutrow.ap, in_=self.win["b_out"].rearrange("(a d) -> a d", a=1))),
             writes=[boutrow.key])
        ln1bc = ar.alloc("ln1bc", [128, D], F32)
        for half in range(2):
            p.op("PE", (lambda e, b=self.pb[5], half=half:
                        e.matmul(b[:, 0:TT], self.onesk1.h[0:1, 0:128],
                                 ln1row.h[0:1, half * 512:(half + 1) * 512],
                                 start=True, stop=True)),
                 reads=[self.onesk1.key, ln1row.key], writes=["PS5"], tag="ln1bc")
            p.op("DVE", (lambda e, b=self.pb[5], half=half:
                         e.tensor_copy(ln1bc.h[:, half * 512:(half + 1) * 512],
                                       b[:, 0:TT])),
                 reads=["PS5"], writes=[ln1bc.key])
        cost = ar.alloc("cost", [128, TT], F32)
        sint = ar.alloc("sint", [128, TT], F32)
        catk = [ar.alloc(f"cdcat{k}", [128, TT], BF16) for k in range(NCH)]
        xa2 = [ar.alloc("cdx", [128, D], F32)] * 2
        x1t2 = [ar.alloc("cdx1", [128, D], F32)] * 2
        y0t2 = [ar.alloc("cdy0", [128, D], F32)] * 2
        lnt2 = [ar.alloc("cdlnt", [128, D], F32)] * 2
        lnab2 = [ar.alloc(f"cdlna{i}", [128, D], BF16) for i in range(2)]
        lnsc2 = [self._mk_ln_scratch(ar, "cd")] * 2
        lnaf = [ar.alloc(f"cdlnaf{k}", [128, TT], BF16) for k in range(NCH)]
        kvlat = [ar.alloc(f"cdkvl{k}", [128, TT], BF16) for k in range(2)]
        qlat = [ar.alloc(f"cdql{k}", [128, TT], BF16) for k in range(2)]
        vts = [ar.alloc(f"cdvts{i}", [128, NH * 65], BF16) for i in range(2)]
        for i in range(2):
            p.op("DVE", (lambda e, i=i: e.memset(
                vts[i].h[:, 64:NH * 65:65], 1.0)), writes=[(vts[i].key, "ones")])
        pre2 = [ar.alloc(f"cdpre{i}", [128, TT], BF16) for i in range(2)]
        rt12 = [ar.alloc("cdrt1", [128, TT], F32)] * 2
        kqf2 = [ar.alloc(f"cdkqf{i}", [128, TT], BF16) for i in range(2)]

        pendT = None
        yield
        for tt in range(NTT):
            p.op("QSP", (lambda e, tt=tt: e.dma_start(
                out=cost.ap, in_=self.cst["cosT"][:, tt * TT:(tt + 1) * TT])),
                 writes=[cost.key], tag="cosld")
            p.op("QSP", (lambda e, tt=tt: e.dma_start(
                out=sint.ap, in_=self.cst["sinT"][:, tt * TT:(tt + 1) * TT])),
                 writes=[sint.key], tag="sinld")
            for k in range(NCH):
                p.op("QSP", (lambda e, k=k, tt=tt: e.dma_start(
                    out=catk[k].ap,
                    in_=self.dr["cat_o"][k, :, tt * TT:(tt + 1) * TT])),
                     reads=[("cat_o",)], writes=[catk[k].key], tag="cdcatld")
            for s in range(4):
                t0 = tt * TT + s * 128
                xa, x1t, y0t = xa2[s % 2], x1t2[s % 2], y0t2[s % 2]
                lnt, lnab, lnsc = lnt2[s % 2], lnab2[s % 2], lnsc2[s % 2]
                p.op("QSP", (lambda e, t0=t0, a=xa.ap: e.dma_start(
                    out=a, in_=self.xin[t0:t0 + 128, :])),
                     writes=[xa.key], tag="cdxld")
                for oc in range(2):
                    bank = oc
                    for k in range(NCH):
                        p.op("PE", (lambda e, b=self.pb[bank], k=k, s=s, oc=oc:
                                    e.matmul(b[:, 0:TT],
                                             catk[k].h[:, s * 128:(s + 1) * 128],
                                             w_outT[k].h[:, oc * 512:
                                                         (oc + 1) * 512],
                                             start=(k == 0), stop=False)),
                             reads=[catk[k].key, w_outT[k].key],
                             writes=[f"PS{bank}"], tag="ow_mm")
                    p.op("PE", (lambda e, b=self.pb[bank], oc=oc:
                                e.matmul(b[:, 0:TT], self.onesk1.h[0:1, 0:128],
                                         boutrow.h[0:1, oc * 512:(oc + 1) * 512],
                                         start=False, stop=True)),
                         reads=[self.onesk1.key, boutrow.key],
                         writes=[f"PS{bank}"], tag="ow_b")
                    p.op("DVE", (lambda e, b=self.pb[bank], oc=oc, x1t=x1t,
                                 xa=xa:
                                 e.tensor_add(x1t.h[:, oc * 512:(oc + 1) * 512],
                                              b[:, 0:TT],
                                              xa.h[:, oc * 512:(oc + 1) * 512])),
                         reads=[f"PS{bank}", xa.key], writes=[x1t.key],
                         tag="x1ev")
                p.op("QACT", (lambda e, t0=t0, a=x1t.ap: e.dma_start(
                    out=self.dr["x1"][t0:t0 + 128, :], in_=a)),
                     reads=[x1t.key], writes=[("x1",)], tag="x1st")
                self._ln_tok(x1t.ap, x1t.key, y0t.ap, y0t.key, lnsc, tag="lny0")
                p.op("QACT", (lambda e, t0=t0, a=y0t.ap: e.dma_start(
                    out=self.dr["y0"][t0:t0 + 128, :], in_=a)),
                     reads=[y0t.key], writes=[("y0",)], tag="y0st")
                self._ln_tok(y0t.ap, y0t.key, lnab.ap, lnab.key, lnsc,
                             wmul=(ln1bc, lnt), tag="lna")
                if pendT is not None:
                    pendT()
                pendT = (lambda lnab=lnab, s=s:
                         self._transpose_tok_to_f(lnab, lnab.key, lnaf,
                                                  s * 128, tag="lnaT"))
                yield
            pendT()
            pendT = None
            lnaf_reads = [(lnaf[k].key, "q", s * 128)
                          for k in range(NCH) for s in range(4)]
            for m2 in range(2):
                for wt, dst in ((kv_dT, kvlat), (q_dT, qlat)):
                    bank = 2 + m2
                    for k in range(NCH):
                        p.op("PE", (lambda e, b=self.pb[bank], k=k, m2=m2, wt=wt:
                                    e.matmul(b[:, 0:TT],
                                             wt[k].h[:, m2 * 128:(m2 + 1) * 128],
                                             lnaf[k].ap, start=(k == 0),
                                             stop=(k == NCH - 1))),
                             reads=[wt[k].key] + lnaf_reads,
                             writes=[f"PS{bank}"], tag="lat_mm")
                    p.op("ACT", (lambda e, b=self.pb[bank], d=dst[m2]:
                                 e.copy(d.ap, b[:, 0:TT])),
                         reads=[f"PS{bank}"], writes=[dst[m2].key], tag="lat_ev")
            for s in range(4):
                vb = vts[s % 2]
                for nb in range(2):
                    bank = 2 + nb
                    for k2 in range(2):
                        p.op("PE", (lambda e, b=self.pb[bank], k2=k2, s=s, nb=nb:
                                    e.matmul(b[:, 0:TT],
                                             kvlat[k2].h[:, s * 128:(s + 1) * 128],
                                             v_uT[k2].h[:, nb * 512:(nb + 1) * 512],
                                             start=(k2 == 0), stop=(k2 == 1))),
                             reads=[kvlat[k2].key, v_uT[k2].key],
                             writes=[f"PS{bank}"], tag="v_mm")
                    p.op("DVE", (lambda e, b=self.pb[bank], vb=vb, nb=nb:
                                 e.tensor_copy(
                                     vb.h[:, nb * 8 * 65:(nb + 1) * 8 * 65]
                                     .rearrange("p (h x) -> p h x", x=65)[:, :, 0:64],
                                     b[:, 0:TT])),
                         reads=[f"PS{bank}"], writes=[vb.key], tag="v_ev")
                p.op("QACT", (lambda e, vb=vb, jt=tt * 4 + s: e.dma_start(
                    out=self.dr["vti"][jt], in_=vb.ap)),
                     reads=[vb.key, (vb.key, "ones")], writes=[("vti",)],
                     tag="vti_st")
            yield
            for c in range(NCH):
                for wi, (which, dst) in enumerate((("k", self.dr["kfull"]),
                                                   ("q", self.dr["qfull"]))):
                    par = (2 * c + wi) % 2
                    pre_sb = pre2[par]
                    rt1 = rt12[par]
                    kqf = kqf2[par]
                    bank = 4 if par == 0 else 2
                    if which == "k":
                        for k2 in range(2):
                            p.op("PE", (lambda e, b=self.pb[bank], c=c, k2=k2:
                                        e.matmul(b[:, 0:TT], KUSEL[c][k2].ap,
                                                 kvlat[k2].h[:, :],
                                                 start=(k2 == 0), stop=False)),
                                 reads=[KUSEL[c][k2].key, kvlat[k2].key],
                                 writes=[f"PS{bank}"], tag="kpre_mm")
                        for k in range(NCH):
                            p.op("PE", (lambda e, b=self.pb[bank], c=c, k=k:
                                        e.matmul(b[:, 0:TT], RKSEL[c][k].ap,
                                                 lnaf[k].ap, start=False,
                                                 stop=(k == NCH - 1))),
                                 reads=[RKSEL[c][k].key] + lnaf_reads,
                                 writes=[f"PS{bank}"], tag="kpre_mm2")
                    else:
                        for k2 in range(2):
                            p.op("PE", (lambda e, b=self.pb[bank], c=c, k2=k2:
                                        e.matmul(b[:, 0:TT],
                                                 QCOMB[k2].h[:, c * 128:
                                                             (c + 1) * 128],
                                                 qlat[k2].h[:, :],
                                                 start=(k2 == 0), stop=(k2 == 1))),
                                 reads=[QCOMB[k2].key, qlat[k2].key],
                                 writes=[f"PS{bank}"], tag="qpre_mm")
                    rbank = 5 if par == 0 else 3
                    p.op("ACT", (lambda e, b=self.pb[bank], a=pre_sb.ap:
                                 e.copy(a, b[:, 0:TT])),
                         reads=[f"PS{bank}"], writes=[pre_sb.key], tag="pre_ev")
                    p.op("PE", (lambda e, b=self.pb[rbank], a=pre_sb.ap:
                                e.matmul(b[:, 0:TT], self.rmat.ap, a,
                                         start=True, stop=True)),
                         reads=[self.rmat.key, pre_sb.key], writes=[f"PS{rbank}"],
                         tag="rot_mm")
                    p.op("DVE", (lambda e, a=rt1.ap, b2=pre_sb.ap:
                                 e.tensor_mul(a, b2, cost.ap)),
                         reads=[pre_sb.key, cost.key], writes=[rt1.key],
                         tag="rope1")
                    p.op("DVE", (lambda e, b=self.pb[rbank]:
                                 e.tensor_mul(b[:, 0:TT], b[:, 0:TT], sint.ap)),
                         reads=[sint.key, f"PS{rbank}"], writes=[f"PS{rbank}"],
                         tag="rope2")
                    p.op("DVE", (lambda e, b=self.pb[rbank], a=kqf.ap, r=rt1.ap:
                                 e.tensor_add(a, r, b[:, 0:TT])),
                         reads=[rt1.key, f"PS{rbank}"], writes=[kqf.key],
                         tag="rope3")
                    p.op("QACT", (lambda e, dst=dst, c=c, tt=tt, a=kqf.ap:
                                  e.dma_start(
                        out=dst[c, :, tt * TT:(tt + 1) * TT], in_=a)),
                         reads=[kqf.key],
                         writes=[("kfull",) if which == "k" else ("qfull",)],
                         tag="kqf_st")
                if wi == 1 and c % 2 == 1:
                    yield
        self.dbg_dump_dram("x1", self.dr["x1"], [("x1",)])
        self.dbg_dump_dram("y0", self.dr["y0"], [("y0",)])
        self.dbg_dump_dram("kfull", self.dr["kfull"], [("kfull",)])
        self.dbg_dump_dram("qfull", self.dr["qfull"], [("qfull",)])
        self.dbg_dump_dram("vti", self.dr["vti"], [("vti",)])

    def phase_E_iter(self):
        nc, p, ar = self.nc, self.p, self.ar
        kch = [ar.alloc(f"ekc{i}", [128, T], BF16) for i in range(2)]
        vti = [ar.alloc(f"evt{j}", [128, NH * 65], BF16) for j in range(T // 128)]
        qch = [ar.alloc(f"eq{i}", [128, TT], BF16) for i in range(2)]
        ptile = [ar.alloc(f"ep{i}", [128, TT], BF16) for i in range(6)]
        catb = [ar.alloc(f"ecat{i}", [64, TT], BF16) for i in range(4)]
        lrow = [ar.alloc(f"elr{i}", [1, TT], F32) for i in range(4)]
        pend_tail = []
        yield
        for qg in range(NTT):
            for j in range(4):
                jt = qg * 4 + j
                p.op("QSP", (lambda e, jt=jt: e.dma_start(
                    out=vti[jt].ap, in_=self.dr["vti"][jt])),
                     reads=[("vti",)], writes=[vti[jt].key], tag="evtld")
            for c in range(NCH):
                qb = qch[c % 2]
                kb = kch[c % 2]
                p.op("QSP", (lambda e, kb=kb, c=c, qg=qg: e.dma_start(
                    out=kb.h[:, 0:(qg + 1) * TT],
                    in_=self.dr["kfull"][c, :, 0:(qg + 1) * TT])),
                     reads=[("kfull",)], writes=[kb.key], tag="ekld")
                p.op("QSP", (lambda e, qb=qb, c=c, qg=qg: e.dma_start(
                    out=qb.ap, in_=self.dr["qfull"][c, :, qg * TT:(qg + 1) * TT])),
                     reads=[("qfull",)], writes=[qb.key], tag="eqld")
                njt = 4 * qg + 4

                def _qk(jt, c=c, qb=qb, kb=kb, qg=qg):
                    s0 = jt - 4 * qg
                    for half in range(2):
                        sb = 2 * (jt % 2) + half
                        hh = half * 64
                        pt = ptile[2 * (jt % 3) + half]
                        p.op("PE", (lambda e, b=self.pb[sb], kb=kb, hh=hh,
                                    jt=jt, qb=qb:
                                    e.matmul(b[:, 0:TT],
                                             kb.h[hh:hh + 64,
                                                  jt * 128:(jt + 1) * 128],
                                             qb.h[hh:hh + 64, :],
                                             start=True, stop=True)),
                             reads=[kb.key, qb.key], writes=[f"PS{sb}"],
                             tag="iqk")
                        p.op("ACT", (lambda e, b=self.pb[sb], pt=pt:
                                     e.activation(pt.ap, b[:, 0:TT], AF.Exp,
                                                  scale=self.sc18.ap)),
                             reads=[f"PS{sb}", self.sc18.key], writes=[pt.key],
                             tag="iexp")
                        if jt >= 4 * qg:
                            if s0 > 0:
                                p.op("DVE", (lambda e, pt=pt, s0=s0: e.memset(
                                    pt.h[:, 0:s0 * 128], 0.0)),
                                     writes=[pt.key], tag="izero")
                            p.op("DVE", (lambda e, pt=pt, s0=s0:
                                         e.tensor_mul(
                                             pt.h[:, s0 * 128:(s0 + 1) * 128],
                                             pt.h[:, s0 * 128:(s0 + 1) * 128],
                                             self.mdiag.ap)),
                                 reads=[self.mdiag.key], writes=[pt.key],
                                 tag="imask")

                def _av(jt, c=c, njt=njt):
                    for half in range(2):
                        ab = 4 + half
                        h = 2 * c + half
                        pt = ptile[2 * (jt % 3) + half]
                        p.op("PE", (lambda e, b=self.pb[ab], jt=jt, h=h, pt=pt,
                                    first=(jt == 0), last=(jt == njt - 1):
                                    e.matmul(b[0:65, 0:TT],
                                             vti[jt].h[:, h * 65:h * 65 + 65],
                                             pt.ap, start=first, stop=last)),
                             reads=[vti[jt].key, pt.key], writes=[f"PS{ab}"],
                             tag="iav")

                for jt in range(njt + 2):
                    if jt < njt:
                        _qk(jt)
                    if jt == 3 and pend_tail:
                        for f in pend_tail:
                            f()
                        pend_tail = []
                    if jt >= 2:
                        _av(jt - 2)
                for half in range(2):
                    ab = 4 + half
                    spb = 2 * (njt % 2) + half
                    hh = half * 64
                    cb = catb[2 * (c % 2) + half]
                    lr = lrow[2 * (c % 2) + half]
                    p.op("ACT", (lambda e, b=self.pb[ab], cb=cb:
                                 e.copy(cb.ap, b[0:64, 0:TT])),
                         reads=[f"PS{ab}"], writes=[(cb.key, "raw")], tag="icat")
                    p.op("DVE", (lambda e, b=self.pb[ab], lr=lr:
                                 e.tensor_copy(lr.ap, b[64:65, 0:TT])),
                         reads=[f"PS{ab}"], writes=[(lr.key, "l")], tag="ilcp")
                    p.op("DVE", (lambda e, lr=lr: e.reciprocal(lr.ap, lr.ap)),
                         reads=[(lr.key, "l")], writes=[lr.key], tag="ilinv")

                    def _tail(spb=spb, cb=cb, lr=lr, c=c, hh=hh, qg=qg):
                        p.op("PE", (lambda e, b=self.pb[spb], lr=lr:
                                    e.matmul(b[0:64, 0:TT],
                                             self.onesk1.h[0:1, 0:64],
                                             lr.ap, start=True, stop=True)),
                             reads=[self.onesk1.key, lr.key],
                             writes=[f"PS{spb}"], tag="ibc")
                        p.op("DVE", (lambda e, b=self.pb[spb], cb=cb:
                                     e.tensor_mul(cb.ap, cb.ap, b[0:64, 0:TT])),
                             reads=[f"PS{spb}", (cb.key, "raw")],
                             writes=[cb.key], tag="inorm")
                        p.op("QACT", (lambda e, cb=cb, c=c, hh=hh, qg=qg:
                                      e.dma_start(
                                          out=self.dr["cat_i"][c, hh:hh + 64,
                                                               qg * TT:
                                                               (qg + 1) * TT],
                                          in_=cb.ap)),
                             reads=[cb.key], writes=[("cat_i",)], tag="icatst")
                    pend_tail.append(_tail)
                if c == NCH - 1:
                    for f in pend_tail:
                        f()
                    pend_tail = []
                yield
        self.dbg_dump_dram("cat_i", self.dr["cat_i"], [("cat_i",)])

    def phase_F_iter(self):
        nc, p, ar = self.nc, self.p, self.ar
        o_wT = [ar.alloc(f"owT{k}", [128, D], BF16) for k in range(NCH)]
        for k in range(NCH):
            p.op("QACT", (lambda e, a=o_wT[k].ap,
                          b=self.dr["ow_bf"][:, k * 128:(k + 1) * 128]:
                          e.dma_start_transpose(a, b)),
                 reads=[("cast", "ow_bf")], writes=[o_wT[k].key], tag="owT")
        rnat = ar.alloc("rnat", [7, D], BF16)
        p.op("QACT", (lambda e: e.dma_start(out=rnat.ap,
                                            in_=self.dr["router_bf"])),
             reads=[("cast", "router_bf")], writes=[rnat.key])
        rwT = [ar.alloc(f"rwT{k}", [128, NE], BF16) for k in range(NCH)]
        for k in range(NCH):
            bank = 6 + k % 2
            p.op("PE", (lambda e, b=self.pb[bank], k=k:
                        e.transpose(b[:, 0:NE],
                                    rnat.h[0:NE, k * 128:(k + 1) * 128],
                                    self.ident.h[0:NE, 0:NE])),
                 reads=[rnat.key, self.ident.key], writes=[f"PS{bank}"])
            p.op("DVE", (lambda e, b=self.pb[bank], k=k:
                         e.tensor_copy(rwT[k].ap, b[:, 0:NE])),
                 reads=[f"PS{bank}"], writes=[rwT[k].key])
        biasrow = ar.alloc("biasrow", [1, NE], F32)
        p.op("QSP", (lambda e: e.dma_start(
            out=biasrow.ap,
            in_=self.win["routing_bias"].rearrange("(a d) -> a d", a=1))),
             writes=[biasrow.key])
        ln2row = ar.alloc("ln2row", [1, D], F32)
        p.op("QSP", (lambda e: e.dma_start(
            out=ln2row.ap, in_=self.win["ln2_w"].rearrange("(a d) -> a d", a=1))),
             writes=[ln2row.key])
        ln2bc = ar.alloc("ln2bc", [128, D], F32)
        for half in range(2):
            p.op("PE", (lambda e, b=self.pb[5], half=half:
                        e.matmul(b[:, 0:TT], self.onesk1.h[0:1, 0:128],
                                 ln2row.h[0:1, half * 512:(half + 1) * 512],
                                 start=True, stop=True)),
                 reads=[self.onesk1.key, ln2row.key], writes=["PS5"])
            p.op("DVE", (lambda e, b=self.pb[5], half=half:
                         e.tensor_copy(ln2bc.h[:, half * 512:(half + 1) * 512],
                                       b[:, 0:TT])),
                 reads=["PS5"], writes=[ln2bc.key])
        catk = [ar.alloc(f"fcat{k}", [128, TT], BF16) for k in range(NCH)]
        y0t = ar.alloc("fy0", [128, D], F32)
        x1a = ar.alloc("fx1", [128, D], F32)
        y1t = ar.alloc("fy1", [128, D], F32)
        baset = ar.alloc("fbase", [128, D], F32)
        lnt = ar.alloc("flnt", [128, D], F32)
        xtt2 = [ar.alloc(f"fxtt{i}", [128, D], BF16) for i in range(2)]
        lnsc = self._mk_ln_scratch(ar, "f")
        xtf = [ar.alloc(f"fxtf{k}", [128, TT], BF16) for k in range(NCH)]
        probs = ar.alloc("fprobs", [128, NE], F32)
        eqm = ar.alloc("feqm", [128, NE], F32)
        wraw = ar.alloc("fwraw", [128, NE], F32)
        m1 = ar.alloc("fm1", [128, 1], F32)
        m2 = ar.alloc("fm2", [128, 1], F32)
        den = ar.alloc("fden", [128, 1], F32)
        w7 = ar.alloc("fw7", [7, 128], F32)

        def _xt_router(xtt, s, tidx):
                self._transpose_tok_to_f(xtt, xtt.key, xtf, s * 128, tag="xtT")
                xtf_reads = [(xtf[k].key, "q", s * 128) for k in range(NCH)]
                bank = 4
                for k in range(NCH):
                    p.op("PE", (lambda e, b=self.pb[bank], k=k, s=s:
                                e.matmul(b[:, 0:NE],
                                         xtf[k].h[:, s * 128:(s + 1) * 128],
                                         rwT[k].ap, start=(k == 0), stop=False)),
                         reads=[rwT[k].key] + xtf_reads, writes=[f"PS{bank}"],
                         tag="rt_mm")
                p.op("PE", (lambda e, b=self.pb[bank]:
                            e.matmul(b[:, 0:NE], self.onesk1.h[0:1, 0:128],
                                     biasrow.ap, start=False, stop=True)),
                     reads=[self.onesk1.key, biasrow.key], writes=[f"PS{bank}"],
                     tag="rt_b")
                p.op("ACT", (lambda e, b=self.pb[bank]:
                             e.activation(probs.ap, b[:, 0:NE], AF.Sigmoid)),
                     reads=[f"PS{bank}"], writes=[probs.key], tag="rt_sig")
                p.op("DVE", (lambda e: e.reduce_max(m1.ap, probs.ap, axis=AX.X)),
                     reads=[probs.key], writes=[m1.key])
                p.op("DVE", (lambda e: e.tensor_scalar(
                    eqm.ap, probs.ap, m1.ap, None, ALU.is_equal)),
                     reads=[probs.key, m1.key], writes=[(eqm.key, "e")])
                p.op("DVE", (lambda e: e.tensor_scalar(
                    eqm.ap, eqm.ap, -1e30, None, ALU.mult)),
                     reads=[(eqm.key, "e")], writes=[eqm.key])
                p.op("DVE", (lambda e: e.tensor_add(eqm.ap, eqm.ap, probs.ap)),
                     reads=[probs.key, eqm.key], writes=[(eqm.key, "p2")])
                p.op("DVE", (lambda e: e.reduce_max(m2.ap, eqm.ap, axis=AX.X)),
                     reads=[(eqm.key, "p2")], writes=[m2.key])
                p.op("DVE", (lambda e: e.tensor_scalar(
                    wraw.ap, probs.ap, m2.ap, None, ALU.is_ge)),
                     reads=[probs.key, m2.key], writes=[wraw.key])
                p.op("DVE", (lambda e: e.tensor_mul(wraw.ap, wraw.ap, probs.ap)),
                     reads=[probs.key], writes=[wraw.key])
                p.op("DVE", (lambda e: e.reduce_sum(den.ap, wraw.ap, axis=AX.X)),
                     reads=[wraw.key], writes=[(den.key, "d")])
                p.op("DVE", (lambda e: e.reciprocal(den.ap, den.ap)),
                     reads=[(den.key, "d")], writes=[den.key])
                p.op("DVE", (lambda e: e.tensor_scalar(
                    wraw.ap, wraw.ap, den.ap, None, ALU.mult)),
                     reads=[den.key, wraw.key], writes=[wraw.key], tag="rt_w")
                p.op("QACT", (lambda e, tidx=tidx: e.dma_start(
                    out=self.dr["wgt_tm"][tidx], in_=wraw.ap)),
                     reads=[wraw.key], writes=[("wgt_tm",)], tag="wgttmst")
                bank5 = 5
                p.op("PE", (lambda e, b=self.pb[bank5]:
                            e.transpose(b[0:NE, 0:128], wraw.ap,
                                        self.ident32.ap)),
                     reads=[wraw.key, self.ident32.key], writes=[f"PS{bank5}"],
                     tag="rt_T")
                p.op("DVE", (lambda e, b=self.pb[bank5]:
                             e.tensor_copy(w7.ap, b[0:NE, 0:128])),
                     reads=[f"PS{bank5}"], writes=[w7.key])
                p.op("QACT", (lambda e, tidx=tidx: e.dma_start(
                    out=self.dr["wgt"][:, tidx * 128:(tidx + 1) * 128],
                    in_=w7.ap)),
                     reads=[w7.key], writes=[("wgt",)], tag="wgtst")

        pendR = None
        yield
        for tt in range(NTT):
            for k in range(NCH):
                p.op("QSP", (lambda e, k=k, tt=tt: e.dma_start(
                    out=catk[k].ap,
                    in_=self.dr["cat_i"][k, :, tt * TT:(tt + 1) * TT])),
                     reads=[("cat_i",)], writes=[catk[k].key], tag="fcatld")
            for s in range(4):
                t0 = tt * TT + s * 128
                tidx = tt * 4 + s
                p.op("QSP", (lambda e, t0=t0: e.dma_start(
                    out=y0t.ap, in_=self.dr["y0"][t0:t0 + 128, :])),
                     reads=[("y0",)], writes=[y0t.key], tag="fy0ld")
                p.op("QSP", (lambda e, t0=t0: e.dma_start(
                    out=x1a.ap, in_=self.dr["x1"][t0:t0 + 128, :])),
                     reads=[("x1",)], writes=[x1a.key], tag="fx1ld")
                for oc in range(2):
                    bank = oc
                    for k in range(NCH):
                        p.op("PE", (lambda e, b=self.pb[bank], k=k, s=s, oc=oc:
                                    e.matmul(b[:, 0:TT],
                                             catk[k].h[:, s * 128:(s + 1) * 128],
                                             o_wT[k].h[:, oc * 512:(oc + 1) * 512],
                                             start=(k == 0), stop=(k == NCH - 1))),
                             reads=[catk[k].key, o_wT[k].key],
                             writes=[f"PS{bank}"], tag="iow_mm")
                    p.op("DVE", (lambda e, b=self.pb[bank], oc=oc:
                                 e.tensor_add(y1t.h[:, oc * 512:(oc + 1) * 512],
                                              b[:, 0:TT],
                                              y0t.h[:, oc * 512:(oc + 1) * 512])),
                         reads=[f"PS{bank}", y0t.key], writes=[y1t.key],
                         tag="y1ev")
                p.op("DVE", (lambda e: e.tensor_add(baset.ap, y1t.ap, x1a.ap)),
                     reads=[y1t.key, x1a.key], writes=[baset.key], tag="basev")
                p.op("QACT", (lambda e, t0=t0: e.dma_start(
                    out=self.out[t0:t0 + 128, :], in_=baset.ap)),
                     reads=[baset.key], writes=[("acc", tidx, 0), ("acc", tidx, 1)],
                     tag="basest")
                xtt = xtt2[s % 2]
                self._ln_tok(y1t.ap, y1t.key, xtt.ap, xtt.key, lnsc,
                             wmul=(ln2bc, lnt), tag="lnxt")
                if pendR is not None:
                    pendR()
                pendR = (lambda xtt=xtt, s=s, tidx=tidx: _xt_router(xtt, s, tidx))
                yield
            pendR()
            pendR = None

            for k in range(NCH):
                p.op("QACT", (lambda e, k=k, tt=tt: e.dma_start(
                    out=self.dr["xtf"][k, :, tt * TT:(tt + 1) * TT],
                    in_=xtf[k].ap)),
                     reads=[(xtf[k].key, "q", s2 * 128) for s2 in range(4)],
                     writes=[("xtf",)], tag="xtfst")
            yield
        self.dbg_dump_dram("xtf", self.dr["xtf"], [("xtf",)])
        self.dbg_dump_dram("wgt", self.dr["wgt"], [("wgt",)])
        if "base" in self.dbg:
            t = self.dbg_out("base", [T, D], F32)
            p.op("QSP", (lambda e, t=t: e.dma_start(out=t, in_=self.out)),
                 reads=[("acc", tidx2, oc) for tidx2 in range(32)
                        for oc in range(2)], tag="dbgbase")

    def phase_G(self):
        nc, p, ar = self.nc, self.p, self.ar
        ar.mark()
        gT2 = [[ar.alloc(f"ggT{pp}_{k}", [128, IE], BF16) for k in range(NCH)]
               for pp in range(2)]
        uT2 = [[ar.alloc(f"guT{pp}_{k}", [128, IE], BF16) for k in range(NCH)]
               for pp in range(2)]
        dT = [ar.alloc(f"gdT{k}", [128, D], BF16) for k in range(IE // 128)]
        xtfk2 = [[ar.alloc(f"gxt{pp}_{k}", [128, TT], BF16) for k in range(NCH)]
                 for pp in range(2)]
        h_sb = [ar.alloc(f"gh{m}", [128, TT], BF16) for m in range(IE // 128)]
        gs = ar.alloc("ggs", [128, TT], BF16)
        wtm = ar.alloc("gwtm", [128, 32 * NE], F32)
        accb2 = [ar.alloc(f"gacc{i}", [128, TT], F32) for i in range(2)]
        p.op("QPOOL", (lambda e: e.dma_start(
            out=wtm.h[:, :].rearrange("p (t e) -> p t e", e=NE),
            in_=self.dr["wgt_tm"].rearrange("t p e -> p t e"),
            allow_slow_non_contiguous=True)),
             reads=[("wgt_tm",)], writes=[wtm.key], tag="gwtmld")
        for eidx, e_i in enumerate(self.moe_experts):
            gT, uT = gT2[eidx % 2], uT2[eidx % 2]
            routed = e_i > 0
            if not routed:
                gsrc, usrc, dsrc = (self.dr["sg_bf"], self.dr["su_bf"],
                                    self.dr["sd_bf"])
                gck, uck, dck = (("cast", "sg_bf"), ("cast", "su_bf"),
                                 ("cast", "sd_bf"))
            else:
                ee = e_i - 1
                gsrc, usrc, dsrc = (self.dr["rg_bf"][ee], self.dr["ru_bf"][ee],
                                    self.dr["rd_bf"][ee])
                gck, uck, dck = (("cast", "rg_bf", ee), ("cast", "ru_bf", ee),
                                 ("cast", "rd_bf", ee))
            for k in range(NCH):
                p.op("QSP", (lambda e, a=gT[k].ap,
                             b=gsrc[:, k * 128:(k + 1) * 128], ck=gck:
                             e.dma_start_transpose(a, b)),
                     reads=[gck], writes=[gT[k].key], tag="gTld")
                p.op("QSP", (lambda e, a=uT[k].ap,
                             b=usrc[:, k * 128:(k + 1) * 128], ck=uck:
                             e.dma_start_transpose(a, b)),
                     reads=[uck], writes=[uT[k].key], tag="uTld")
            for k in range(IE // 128):
                p.op("QSP", (lambda e, a=dT[k].ap,
                             b=dsrc[:, k * 128:(k + 1) * 128], ck=dck:
                             e.dma_start_transpose(a, b)),
                     reads=[dck], writes=[dT[k].key], tag="dTld")
            for tt in range(NTT):
                xtfk = xtfk2[tt % 2]
                for k in range(NCH):
                    p.op("QPOOL", (lambda e, a=xtfk[k].ap, k=k, tt=tt: e.dma_start(
                        out=a,
                        in_=self.dr["xtf"][k, :, tt * TT:(tt + 1) * TT])),
                         reads=[("xtf",)], writes=[xtfk[k].key], tag="gxld")
                for m in range(IE // 128):
                    gb, ub = m % 2 * 2, m % 2 * 2 + 1
                    for k in range(NCH):
                        p.op("PE", (lambda e, b=self.pb[gb], k=k, m=m,
                                    w=gT[k], xt=xtfk[k]:
                                    e.matmul(b[:, 0:TT],
                                             w.h[:, m * 128:(m + 1) * 128],
                                             xt.ap, start=(k == 0),
                                             stop=(k == NCH - 1))),
                             reads=[gT[k].key, xtfk[k].key], writes=[f"PS{gb}"],
                             tag="g_mm")
                    for k in range(NCH):
                        p.op("PE", (lambda e, b=self.pb[ub], k=k, m=m,
                                    w=uT[k], xt=xtfk[k]:
                                    e.matmul(b[:, 0:TT],
                                             w.h[:, m * 128:(m + 1) * 128],
                                             xt.ap, start=(k == 0),
                                             stop=(k == NCH - 1))),
                             reads=[uT[k].key, xtfk[k].key], writes=[f"PS{ub}"],
                             tag="u_mm")
                    p.op("ACT", (lambda e, b=self.pb[gb]:
                                 e.activation(gs.ap, b[:, 0:TT], AF.Silu)),
                         reads=[f"PS{gb}"], writes=[gs.key], tag="silu")
                    p.op("DVE", (lambda e, b=self.pb[ub], m=m:
                                 e.tensor_mul(h_sb[m].ap, gs.ap, b[:, 0:TT])),
                         reads=[gs.key, f"PS{ub}"], writes=[(h_sb[m].key, "h")],
                         tag="hmul")

                for s in range(4):
                    tidx = tt * 4 + s
                    for oc in range(2):
                        bank = 4 + oc
                        for k in range(IE // 128):
                            p.op("PE", (lambda e, b=self.pb[bank], k=k, s=s,
                                        oc=oc:
                                        e.matmul(b[:, 0:TT],
                                                 h_sb[k].h[:, s * 128:
                                                           (s + 1) * 128],
                                                 dT[k].h[:, oc * 512:
                                                         (oc + 1) * 512],
                                                 start=(k == 0),
                                                 stop=(k == IE // 128 - 1))),
                                 reads=[(h_sb[k].key, "h"), dT[k].key],
                                 writes=[f"PS{bank}"], tag="d_mm")
                        accb = accb2[(s * 2 + oc) % 2]
                        p.op("QPOOL", (lambda e, a=accb.ap, tidx=tidx, oc=oc:
                                       e.dma_start(
                            out=a,
                            in_=self.out[tidx * 128:(tidx + 1) * 128,
                                         oc * 512:(oc + 1) * 512])),
                             reads=[("acc", tidx, oc)], writes=[accb.key],
                             tag="accld")
                        if routed:
                            p.op("DVE", (lambda e, b=self.pb[bank], a=accb.ap,
                                         wc=wtm.h[:, tidx * NE + e_i - 1:
                                                  tidx * NE + e_i]:
                                         e.scalar_tensor_tensor(
                                             a, b[:, 0:TT], wc, a,
                                             ALU.mult, ALU.add)),
                                 reads=[f"PS{bank}", wtm.key],
                                 writes=[accb.key], tag="accadd")
                        else:
                            p.op("DVE", (lambda e, b=self.pb[bank], a=accb.ap:
                                         e.tensor_add(a, b[:, 0:TT], a)),
                                 reads=[f"PS{bank}"], writes=[accb.key],
                                 tag="accadd")
                        p.op("QACT", (lambda e, a=accb.ap, tidx=tidx, oc=oc:
                                      e.dma_start(
                            out=self.out[tidx * 128:(tidx + 1) * 128,
                                         oc * 512:(oc + 1) * 512],
                            in_=a)),
                             reads=[accb.key], writes=[("acc", tidx, oc)],
                             tag="accst")
        ar.release()


def _in_map(core, inputs):
    g = lambda n: np.ascontiguousarray(inputs[n])
    m = {
        "xin": np.ascontiguousarray(inputs["L0_attn_x"][core]),
        "w_qkv": g("L0_attn_w_qkv"),
        "w_conv": np.ascontiguousarray(
            inputs["L0_attn_w_conv"].reshape(D, DH * CF)),
        "b_conv": g("L0_attn_b_conv"),
        "null_k": np.ascontiguousarray(inputs["L0_attn_null_k"].reshape(D)),
        "null_v": np.ascontiguousarray(inputs["L0_attn_null_v"].reshape(D)),
        "w_out": g("L0_attn_w_out"), "b_out": g("L0_attn_b_out"),
        "ln1_w": g("L0_ffn_ln1_w"), "ln2_w": g("L0_ffn_ln2_w"),
        "kv_d": g("L0_ffn_kv_d"), "q_d": g("L0_ffn_q_d"),
        "k_u": g("L0_ffn_k_u"), "q_u": g("L0_ffn_q_u"), "v_u": g("L0_ffn_v_u"),
        "rope_q": g("L0_ffn_rope_q_w"), "rope_k": g("L0_ffn_rope_k_w"),
        "o_w": g("L0_ffn_o_w"), "router_w": g("L0_ffn_router_w"),
        "routing_bias": g("L0_ffn_routing_bias"),
        "sh_gate": np.ascontiguousarray(inputs["L0_ffn_shared_gate"][0]),
        "sh_up": np.ascontiguousarray(inputs["L0_ffn_shared_up"][0]),
        "sh_down": np.ascontiguousarray(inputs["L0_ffn_shared_down"][0]),
        "r_gate": g("L0_ffn_routed_gate"), "r_up": g("L0_ffn_routed_up"),
        "r_down": g("L0_ffn_routed_down"),
    }
    return m


def kernel(**inputs):
    b = Build()
    nc = b.build()
    in_maps = [_in_map(c, inputs) for c in range(8)]
    res = run_bass_kernel_spmd(nc, in_maps, core_ids=list(range(8)))
    return np.stack([res.results[c]["out"] for c in range(8)], axis=0)
